# version 01 (final)
# baseline (speedup 1.0000x reference)
import sys
if "/opt/trn_rl_repo" not in sys.path:
    sys.path.insert(0, "/opt/trn_rl_repo")

from contextlib import ExitStack

import numpy as np
import ml_dtypes

import concourse.bass as bass
import concourse.mybir as mybir
from concourse.bass_utils import run_bass_kernel_spmd

F32 = mybir.dt.float32
BF16 = mybir.dt.bfloat16
I32 = mybir.dt.int32
AF = mybir.ActivationFunctionType
ALU = mybir.AluOpType
AX = mybir.AxisListType
BF = ml_dtypes.bfloat16

NCORES = 8
T = 1024
H = 4096
HC = 32
S = 1024
SC = 8
FF = 9984
FFC = 78
D = 1024
DC = 8
GF = 13
NG = 6
LN_EPS = 1e-5
KB = 1024

ENGS = ("sync", "gpsimd", "scalar", "vector", "tensor")
DEBUG_A = False
DEBUG_B = False
STOP_A = None

XA = 0
XB = 64 * KB
XZ = 128 * KB
MA = 192 * KB
MB = 196 * KB
MC = 200 * KB
ARENA_BYTES = 204 * KB


class Em:

    def __init__(self, nc):
        self.nc = nc
        self.q = {k: [] for k in ENGS}
        self.cnt = {s: 0 for s in ("pe", "ac", "dv")}

    def op(self, eng, fn, sem=None, inc=0):
        if inc:
            self.cnt[sem] = self.cnt.get(sem, 0) + inc
            tgt = self.cnt[sem]

            def wrapped(e, sems, fn=fn, sem=sem, inc=inc):
                fn(e).then_inc(sems[sem], inc)
        else:
            tgt = None

            def wrapped(e, sems, fn=fn):
                fn(e)
        self.q[eng].append(wrapped)
        return tgt

    def wait(self, eng, sem, val):
        if val is None or val <= 0:
            return
        self.q[eng].append(
            lambda e, sems, sem=sem, val=val: e.wait_ge(sems[sem], val))

    def set_fence_cell(self, ap, ap_act=None):
        self._fence = ap
        self._fenca = ap_act

    def bubble(self):
        self.op("vector", lambda e: e.tensor_single_scalar(
            self._fence, self._fence, 1.0, ALU.mult))

    def act_fence(self):
        return self.op("scalar", lambda e: e.mul(self._fenca, self._fenca, 1.0),
                       sem="ac", inc=1)

    def dve_fence(self):
        return self.op("vector", lambda e: e.tensor_single_scalar(
            self._fence, self._fence, 1.0, ALU.mult), sem="dv", inc=1)

    def dmas(self, dst, src, lane):
        return self.op("sync", lambda e, d=dst, s=src: e.dma_start(out=d, in_=s),
                       sem=lane, inc=16)

    def dmag(self, dst, src, lane):
        return self.op("gpsimd", lambda e, d=dst, s=src: e.dma_start(out=d, in_=s),
                       sem=lane, inc=16)

    def mm_group(self, ps_ap, pairs, nsplit=2):
        ncols = ps_ap.shape[-1]
        w = ncols // nsplit
        n = len(pairs)
        tgt = None
        for i, (lh, rh) in enumerate(pairs):
            for j in range(nsplit):
                st = (i == 0)
                sp = (i == n - 1)
                f = (lambda e, p=ps_ap[:, j * w:(j + 1) * w], lh=lh,
                     rh=rh[:, j * w:(j + 1) * w], st=st, sp=sp:
                     e.matmul(p, lh, rh, start=st, stop=sp))
                if sp and j == nsplit - 1:
                    tgt = self.op("tensor", f, sem="pe", inc=1)
                else:
                    self.op("tensor", f)
        return tgt

    def finalize(self, block, sems):
        def mk(name):
            def body(e):
                for f in self.q[name]:
                    f(e, sems)
            return body
        block.sync(mk("sync"))
        block.gpsimd(mk("gpsimd"))
        block.scalar(mk("scalar"))
        block.vector(mk("vector"))
        block.tensor(mk("tensor"))


class Arena:
    def __init__(self, nc, nbytes):
        self.h = nc.alloc_sbuf_tensor("arena", [128, nbytes], mybir.dt.uint8)
        self._f = self.h.ap().bitcast(F32)
        self._b = self.h.ap().bitcast(BF16)

    def f(self, off, n):
        return self._f[:, off // 4: off // 4 + n]

    def b(self, off, n):
        return self._b[:, off // 2: off // 2 + n]


class WStream:

    def __init__(self, em, name, stage_fn, bf_fn, init_dma_wait=None,
                 init_cast_wait=None):
        self.em = em
        self.lanes = (name + "0", name + "1")
        self.stage_fn = stage_fn
        self.bf_fn = bf_fn
        self.i = 0
        self.cast_ms = {}
        self.cons_pe = {}
        self.init_dma_wait = init_dma_wait
        self.init_cast_wait = init_cast_wait

    def job(self, src_ap):
        em = self.em
        j = self.i
        slot = j % 2
        if j == 0 and self.init_dma_wait:
            em.wait("sync", *self.init_dma_wait)
        if j >= 2:
            em.wait("sync", "ac", self.cast_ms[j - 2])
        d = em.dmas(self.stage_fn(slot), src_ap, self.lanes[slot])
        em.wait("scalar", self.lanes[slot], d)
        if j == 0 and self.init_cast_wait:
            em.wait("scalar", *self.init_cast_wait)
        if j >= 2:
            em.wait("scalar", "pe", self.cons_pe.get(j - 2, 0))
        self.cast_ms[j] = em.op(
            "scalar", lambda e, s=slot: e.copy(self.bf_fn(s), self.stage_fn(s)),
            sem="ac", inc=1)
        self.i += 1
        return j, slot, self.cast_ms[j]

    def note_consumed(self, j, pe_ms):
        self.cons_pe[j] = pe_ms


def _pairwise(em, op, src, width, red, out_col, sem=None, inc=0):
    w = width // 2
    em.bubble()
    em.op("vector", lambda e, w=w: e.tensor_tensor(
        red[:, 0:w], src[:, 0:w], src[:, w:2 * w], op))
    while w > 2:
        h = w // 2
        em.bubble()
        em.op("vector", lambda e, h=h, w=w: e.tensor_tensor(
            red[:, 0:h], red[:, 0:h], red[:, h:w], op))
        w = h
    em.bubble()
    return em.op("vector", lambda e: e.tensor_tensor(
        out_col[:, 0:1], red[:, 0:1], red[:, 1:2], op), sem=sem, inc=inc)


def _consts(nc, want):
    out = {}
    for name, shape, dt in want:
        out[name] = nc.alloc_sbuf_tensor("c_" + name, list(shape), dt).ap()
    return out


def _emit_ln(em, ar, cons, ps0, ps1, ps2, src_bf, src_fp, dst_fn, n_oc=HC):
    vcs = [ar.b(MA, T), ar.b(MA + 2 * KB, T)]
    row_m = ar.f(MB, T)
    row_r = ar.f(MC, T)
    invn = 1.0 / H
    p_2 = 0
    p_2_prev = 0
    for oc in range(n_oc):
        vc = vcs[oc % 2]
        src = src_fp(oc) if src_fp is not None else src_bf(oc)
        em.wait("scalar", "pe", p_2_prev)
        em.op("scalar", lambda e, vc=vc, src=src: e.copy(vc, src))
        a_c = em.act_fence()
        em.wait("tensor", "ac", a_c)
        st = (oc == 0)
        sp = (oc == n_oc - 1)
        em.op("tensor", lambda e, st=st, sp=sp, vc=vc:
              e.matmul(ps2[0:1, 0:512], cons["onesb"][:, 0:1],
                       vc[:, 0:512], start=st, stop=sp, skip_group_check=True))
        p_1 = em.op("tensor", lambda e, st=st, sp=sp, vc=vc:
                    e.matmul(ps2[0:1, 512:1024], cons["onesb"][:, 0:1],
                             vc[:, 512:1024], start=st, stop=sp,
                             skip_group_check=True),
                    sem="pe", inc=1)
        em.wait("scalar", "pe", p_1)
        em.op("scalar", lambda e, vc=vc: e.square(vc, vc))
        a_s = em.act_fence()
        em.wait("tensor", "ac", a_s)
        em.op("tensor", lambda e, st=st, sp=sp, vc=vc:
              e.matmul(ps0[0:1, 0:512], cons["onesb"][:, 0:1],
                       vc[:, 0:512], start=st, stop=sp, skip_group_check=True))
        p_2_prev = p_2
        p_2 = em.op("tensor", lambda e, st=st, sp=sp, vc=vc:
                    e.matmul(ps0[0:1, 512:1024], cons["onesb"][:, 0:1],
                             vc[:, 512:1024], start=st, stop=sp,
                             skip_group_check=True),
                    sem="pe", inc=1)
    p_st = p_2
    em.wait("vector", "pe", p_st)
    em.op("vector", lambda e: e.tensor_scalar(row_m[0:1, :], ps2[0:1, 0:T],
                                              invn, None, ALU.mult))
    em.bubble()
    em.op("vector", lambda e: e.tensor_mul(row_r[0:1, :], row_m[0:1, :],
                                           row_m[0:1, :]))
    em.op("vector", lambda e: e.tensor_scalar(row_r[0:1, :], row_r[0:1, :],
                                              -1.0, None, ALU.mult))
    em.op("vector", lambda e: e.scalar_tensor_tensor(
        row_r[0:1, :], ps0[0:1, 0:T], invn, row_r[0:1, :], ALU.mult, ALU.add))
    v_v = em.dve_fence()
    em.wait("scalar", "dv", v_v)
    em.op("scalar", lambda e: e.activation(row_r[0:1, :], row_r[0:1, :],
                                           AF.Ln, bias=cons["epsln"][0:1, 0:1]))
    em.op("scalar", lambda e: e.activation(row_r[0:1, :], row_r[0:1, :],
                                           AF.Exp, scale=-0.5))
    a_q = em.act_fence()
    em.wait("tensor", "ac", a_q)
    for j in (0, 1):
        em.op("tensor", lambda e, j=j: e.matmul(
            ps0[:, j * 512:(j + 1) * 512], cons["ones1"][0:1, :],
            row_m[0:1, j * 512:(j + 1) * 512], start=True, stop=True))
    em.op("tensor", lambda e: e.matmul(
        ps1[:, 0:512], cons["ones1"][0:1, :],
        row_r[0:1, 0:512], start=True, stop=True))
    p_bc = em.op("tensor", lambda e: e.matmul(
        ps1[:, 512:1024], cons["ones1"][0:1, :],
        row_r[0:1, 512:1024], start=True, stop=True), sem="pe", inc=1)
    em.wait("vector", "pe", p_bc)
    last = None
    for oc in range(n_oc):
        dst, ssrc = dst_fn(oc)
        em.op("vector", lambda e, d=dst, s=ssrc: e.tensor_sub(d, s, ps0[:, 0:T]))
        last = em.op("vector", lambda e, d=dst: e.tensor_mul(d, d, ps1[:, 0:T]),
                     sem="dv", inc=1)
    return em.dve_fence()


def _fin(nc, em):
    for lane in list(em.cnt.keys()):
        if lane.startswith(("st", "sp")):
            em.wait("gpsimd", lane, em.cnt[lane])
    sems = {}
    with ExitStack() as sctx:
        for sname in sorted(em.cnt.keys()):
            sems[sname] = sctx.enter_context(nc.semaphore(sname))
        block = sctx.enter_context(nc.Block())
        em.finalize(block, sems)
    return nc



def build_A():
    nc = bass.Bass("TRN2", target_bir_lowering=False,
                   detect_race_conditions=False)

    xT = nc.dram_tensor("xT", [HC, 128, T], F32, kind="ExternalInput").ap()
    upT = nc.dram_tensor("upT", [SC, 128, H], F32, kind="ExternalInput").ap()
    gateT = nc.dram_tensor("gateT", [SC, 128, H], F32, kind="ExternalInput").ap()
    lp_d = nc.dram_tensor("lp", [128, SC], F32, kind="ExternalInput").ap()
    laT = nc.dram_tensor("laT", [HC, 128, S], F32, kind="ExternalInput").ap()
    fuseT = nc.dram_tensor("fuseT", [6, HC, 128, S], F32, kind="ExternalInput").ap()
    crow_d = nc.dram_tensor("crow", [1, 8], F32, kind="ExternalInput").ap()
    gwT = nc.dram_tensor("gwT", [FFC, 128, H], F32, kind="ExternalInput").ap()
    uwT = nc.dram_tensor("uwT", [FFC, 128, H], F32, kind="ExternalInput").ap()
    dwT = nc.dram_tensor("dwT", [HC, 128, FF], F32, kind="ExternalInput").ap()
    wbT = nc.dram_tensor("wbT", [HC, 128, H], F32, kind="ExternalInput").ap()
    bb_d = nc.dram_tensor("bb", [128, HC], F32, kind="ExternalInput").ap()
    AaT = nc.dram_tensor("AaT", [1, 128, H], F32, kind="ExternalInput").ap()
    BaT = nc.dram_tensor("BaT", [HC, 128, 128], F32, kind="ExternalInput").ap()
    rwT_d = nc.dram_tensor("rwT", [128, 512], F32, kind="ExternalInput").ap()
    rb_d = nc.dram_tensor("rbrow", [1, 16], F32, kind="ExternalInput").ap()
    pr_d = nc.dram_tensor("prow", [1, 16], F32, kind="ExternalInput").ap()
    act_d = nc.dram_tensor("act4", [4, 1], I32, kind="ExternalInput").ap()
    arr_d = nc.dram_tensor("arr16", [4, 16], F32, kind="ExternalInput").ap()
    m16_d = nc.dram_tensor("m16", [16, 128], BF16, kind="ExternalInput").ap()
    wqkvT = nc.dram_tensor("wqkvT", [24, 128, H], F32, kind="ExternalInput").ap()
    bqkv_d = nc.dram_tensor("bqkv", [128, 24], F32, kind="ExternalInput").ap()
    idb_d = nc.dram_tensor("identb", [128, 128], BF16, kind="ExternalInput").ap()

    x3o = nc.dram_tensor("x3o", [HC, 128, T], BF16, kind="ExternalOutput").ap()
    qo = nc.dram_tensor("qo", [DC, 128, T], BF16, kind="ExternalOutput").ap()
    ko = nc.dram_tensor("ko", [DC, 128, T], BF16, kind="ExternalOutput").ap()
    vto = nc.dram_tensor("vto", [DC, 128, T], BF16, kind="ExternalOutput").ap()

    act_dram = nc.dram_tensor("act_scr", [FFC, 128, T], BF16, kind="Internal").ap()
    if DEBUG_A:
        dbg_up = nc.dram_tensor("dbg_up", [SC, 128, T], BF16, kind="ExternalOutput").ap()
        dbg_dlt = nc.dram_tensor("dbg_dlt", [SC, 128, T], BF16, kind="ExternalOutput").ap()
        dbg_acc4 = nc.dram_tensor("dbg_acc4", [HC, 128, T], F32, kind="ExternalOutput").ap()
        dbg_tmp = nc.dram_tensor("dbg_tmp", [6, SC, 128, T], BF16, kind="ExternalOutput").ap()
        dbg_acc5 = nc.dram_tensor("dbg_acc5", [HC, 128, T], F32, kind="ExternalOutput").ap()
        dbg_x2 = nc.dram_tensor("dbg_x2", [HC, 128, T], BF16, kind="ExternalOutput").ap()
        dbg_vbf = nc.dram_tensor("dbg_vbf", [HC, 128, T], BF16, kind="ExternalOutput").ap()
        dbg_cb = nc.dram_tensor("dbg_cb", [128, T], F32, kind="ExternalOutput").ap()
        dbg_rt = nc.dram_tensor("dbg_rt", [6, 128, 128], F32, kind="ExternalOutput").ap()
        dbg_hw = nc.dram_tensor("dbg_hw", [128, T], BF16, kind="ExternalOutput").ap()

    with ExitStack() as ctx:
        ar = Arena(nc, ARENA_BYTES)
        ps = [ctx.enter_context(nc.psum_tensor(f"ps{i}", [128, 1024], F32)).ap()
              for i in range(3)]
        pst = [ctx.enter_context(nc.psum_tensor(f"pst{i}", [128, 1024], BF16)).ap()
               for i in range(2)]

        cons = _consts(nc, [
            ("lp", (128, SC), F32), ("nlp", (128, SC), F32),
            ("crow", (1, 8), F32), ("rrow", (1, 8), F32), ("rcol", (128, 8), F32),
            ("bb", (128, HC), F32),
            ("rb", (1, 16), F32), ("pr", (1, 16), F32), ("srow", (1, 16), F32),
            ("act4", (4, 1), I32), ("actf", (4, 1), F32),
            ("arr16", (4, 16), F32), ("eq", (4, 16), F32),
            ("m16", (16, 128), BF16),
            ("srowb", (128, 16), F32), ("mrowb", (128, 16), F32),
            ("bqkv", (128, 24), F32), ("bq32", (128, 8), F32),
            ("ones1", (1, 128), F32), ("onesb", (128, 1), BF16),
            ("identb", (128, 128), BF16),
            ("cols", (128, 8), F32), ("red", (128, 8), F32),
            ("lg", (128, 16), F32), ("emx", (128, 16), F32),
            ("abf", (128, 16), BF16), ("at", (16, 128), BF16),
            ("epsc", (1, 2), F32),
            ("ones4", (4, 128), F32), ("fence", (1, 256), BF16),
        ])
        cons["nmax"] = cons["cols"][:, 0:1]
        cons["sums"] = cons["cols"][:, 1:2]
        cons["rcp"] = cons["cols"][:, 2:3]
        cons["mxc"] = cons["cols"][:, 3:4]
        cons["eps12"] = cons["epsc"][0:1, 0:1]
        cons["epsln"] = cons["epsc"][0:1, 1:2]

        em = Em(nc)
        em.set_fence_cell(cons["fence"][0:1, 0:256], cons["fence"][0:1, 0:256])

        xbf = lambda hc: ar.b(XA + hc * 2048, T)
        acc = lambda oc: ar.f(XA + oc * 4096, T)
        vbf = lambda oc: ar.b(XA + oc * 2048, T)
        x2bf = lambda oc: ar.b(XZ + oc * 2048, T)
        wstage = lambda k, s: ar.f(XB + (k * 2 + s) * 16 * KB, H)
        wbf = lambda k, s: ar.b(XZ + 32 * KB + (k * 2 + s) * 8 * KB, H)
        dlt = lambda s: ar.b(XZ + s * 2048, T)
        upb = lambda s: ar.b(XZ + 16 * KB + s * 2048, T)
        tmp = lambda p, s: ar.b(XZ + (32 + 16 * p) * KB + s * 2048, T)
        fstage = lambda s: ar.f(MA + s * 4 * KB, S)
        fbf = lambda s: ar.b(MC + s * 2 * KB, S)
        xst = [ar.f(MA, T), ar.f(MB, T)]
        sg0 = ar.f(MA, T)
        actout = [ar.b(MC, T), ar.b(MC + 2 * KB, T)]
        actg = lambda i: ar.b(XZ + i * 2048, T)
        dstage = lambda s: ar.f(XZ + 28 * KB + s * 8 * KB, GF * 128)
        dbf = lambda s: ar.b(XZ + 44 * KB + s * 4 * KB, GF * 128)
        wstage7 = lambda s: ar.f(XB + s * 16 * KB, H)
        wbf7 = lambda s: ar.b(XB + 32 * KB + s * 8 * KB, H)
        ba_st = lambda s: ar.f(XB + 48 * KB + s * KB, 128)
        ba_bf = lambda s: ar.b(XB + 50 * KB + s * 512, 128)
        rw_st = ar.f(XB + 52 * KB, 512)
        rw_bf = ar.b(XB + 54 * KB, 512)
        hw = ar.b(MC, T)
        cbx = ar.f(MB, T)
        qch = lambda i: ar.b(XZ + i * 2048, T)
        kch = lambda i: ar.b(XZ + 16 * KB + i * 2048, T)
        vch = lambda i: ar.b(XZ + 32 * KB + i * 2048, T)
        vtch = lambda i: ar.b(XZ + 48 * KB + i * 2048, T)

        ps_free = {0: ("pe", 0), 1: ("pe", 0), 2: ("pe", 0),
                   "tA": ("pe", 0), "tB": ("pe", 0)}

        def psum_take(par):
            sname, sval = ps_free[par]
            em.wait("tensor", sname, sval)

        for dst, src in ((cons["lp"], lp_d), (cons["crow"], crow_d),
                         (cons["bb"], bb_d), (cons["rb"], rb_d),
                         (cons["pr"], pr_d), (cons["act4"], act_d),
                         (cons["arr16"], arr_d), (cons["m16"], m16_d),
                         (cons["bqkv"], bqkv_d)):
            em.dmas(dst[:, :], src[:, :], "cst")
        d_c = em.dmas(cons["identb"][:, :], idb_d[:, :], "cst")
        em.wait("scalar", "cst", d_c)
        em.op("scalar", lambda e: e.mul(cons["nlp"][:, :], cons["lp"][:, :], -1.0))
        em.op("scalar", lambda e: e.mul(cons["bq32"][:, :], cons["bqkv"][:, 0:8],
                                        1.0 / 32.0))
        a_c1 = em.op("scalar", lambda e: e.copy(cons["actf"][:, :], cons["act4"][:, :]),
                     sem="ac", inc=1)
        em.wait("vector", "ac", a_c1)
        em.op("vector", lambda e: e.memset(cons["ones1"][:, :], 1.0))
        em.op("vector", lambda e: e.memset(cons["onesb"][:, :], 1.0))
        em.op("vector", lambda e: e.memset(cons["eps12"][:, :], 1e-12))
        em.op("vector", lambda e: e.memset(cons["ones4"][:, :], 1.0))
        em.op("vector", lambda e: e.memset(cons["rrow"][:, :], 0.0))
        em.op("vector", lambda e: e.memset(cons["fence"][:, :], 1.0))
        em.op("vector", lambda e: e.memset(cons["epsln"][:, :], LN_EPS))
        em.op("vector", lambda e: e.tensor_sub(cons["rrow"][0:1, 0:7],
                                               cons["crow"][0:1, 0:7],
                                               cons["crow"][0:1, 1:8]))
        em.op("vector", lambda e: e.tensor_scalar(cons["eq"][:, :], cons["arr16"][:, :],
                                                  cons["actf"][:, 0:1], None,
                                                  ALU.is_equal))
        v_c1 = em.op("vector", lambda e: e.engine_nop(), sem="dv", inc=1)
        em.wait("scalar", "dv", v_c1)
        em.op("scalar", lambda e: e.activation(cons["srow"][:, :], cons["pr"][:, :],
                                               AF.Ln, bias=cons["eps12"][0:1, 0:1]))
        a_c2 = em.op("scalar", lambda e: e.activation(cons["rrow"][0:1, 0:7],
                                                      cons["rrow"][0:1, 0:7], AF.Exp),
                     sem="ac", inc=1)
        em.wait("vector", "ac", a_c2)
        v_srow = em.op("vector", lambda e: e.tensor_add(cons["srow"][:, :],
                                                        cons["srow"][:, :],
                                                        cons["rb"][:, :]),
                       sem="dv", inc=1)
        em.wait("tensor", "dv", v_srow)
        em.wait("tensor", "ac", a_c2)
        em.op("tensor", lambda e: e.matmul(ps[2][:, 0:8], cons["ones1"][0:1, :],
                                           cons["rrow"][0:1, 0:8], start=True,
                                           stop=True))
        em.op("tensor", lambda e: e.matmul(ps[2][:, 8:24], cons["ones1"][0:1, :],
                                           cons["srow"][0:1, :], start=True, stop=True))
        p_c = em.op("tensor", lambda e: e.matmul(ps[2][:, 32:48], cons["ones4"][:, :],
                                           cons["eq"][:, :], start=True, stop=True),
                    sem="pe", inc=1)
        em.wait("vector", "pe", p_c)
        em.op("vector", lambda e: e.tensor_single_scalar(cons["rcol"][:, :], ps[2][:, 0:8], 1.0, ALU.mult))
        em.op("vector", lambda e: e.tensor_single_scalar(cons["srowb"][:, :], ps[2][:, 8:24], 1.0, ALU.mult))
        v_c2 = em.op("vector", lambda e: e.tensor_single_scalar(cons["mrowb"][:, :],
                                                                ps[2][:, 32:48], 1.0, ALU.mult),
                     sem="dv", inc=1)
        ps_free[2] = ("dv", v_c2)

        xc = [0] * HC
        for i in range(HC):
            lane = "xl%d" % (i % 2)
            if i >= 2:
                em.wait("sync", "ac", xc[i - 2])
            d = em.dmas(xst[i % 2][:, :], xT[i], lane)
            em.wait("scalar", lane, d)
            xc[i] = em.op("scalar", lambda e, i=i: e.copy(xbf(i), xst[i % 2][:, :]),
                          sem="ac", inc=1)
        a_xload = xc[HC - 1]
        if STOP_A == "A1":
            return _fin(nc, em)

        ws_a = WStream(em, "wa", lambda s: wstage(0, s)[:, :], lambda s: wbf(0, s)[:, :])
        ws_b = WStream(em, "wb", lambda s: wstage(1, s)[:, :], lambda s: wbf(1, s)[:, :])
        em.wait("tensor", "ac", a_xload)
        a_A2_dlt = None
        up_done = [None] * SC
        for sc in range(SC):
            ju, su, cu = ws_a.job(upT[sc])
            jg, sg_, cg = ws_b.job(gateT[sc])
            for (j_, wsx, cast, kind) in ((ju, ws_a, cu, 0), (jg, ws_b, cg, 1)):
                par = (2 * sc + kind) % 2
                em.wait("tensor", "ac", cast)
                psum_take(par)
                w = wbf(kind, su if kind == 0 else sg_)
                pairs = [(w[:, hc * 128:(hc + 1) * 128], xbf(hc)) for hc in range(HC)]
                pm = em.mm_group(ps[par], pairs)
                wsx.note_consumed(j_, pm)
                if kind == 0:
                    em.wait("scalar", "pe", pm)
                    em.op("scalar", lambda e, sc=sc, p=ps[par]: e.copy(upb(sc), p))
                    ev = em.act_fence()
                    ps_free[par] = ("ac", ev)
                    up_done[sc] = ev
                else:
                    em.wait("vector", "pe", pm)
                    ev = em.op("vector", lambda e, sc=sc, p=ps[par]:
                               e.tensor_scalar(dlt(sc), p, cons["nlp"][:, sc:sc + 1],
                                               None, ALU.add),
                               sem="dv", inc=1)
                    ps_free[par] = ("dv", ev)
                    a_A2_dlt = ev

        if DEBUG_A:
            for sc_ in range(SC):
                em.wait("gpsimd", "ac", up_done[sc_])
            em.wait("gpsimd", "dv", a_A2_dlt)
            for sc_ in range(SC):
                em.dmag(dbg_up[sc_], upb(sc_), "st")
                em.dmag(dbg_dlt[sc_], dlt(sc_), "st")
        if STOP_A == "A2":
            return _fin(nc, em)

        spill = [0] * FFC
        v_a3_last = None
        for f in range(FFC):
            jg, sg_slot, cg = ws_a.job(gwT[f])
            ju, su_slot, cu = ws_b.job(uwT[f])
            pg, pu = f % 2, 2
            em.wait("tensor", "ac", cg)
            psum_take(pg)
            pmg = em.mm_group(ps[pg], [(wbf(0, sg_slot)[:, hc * 128:(hc + 1) * 128],
                                        xbf(hc)) for hc in range(HC)])
            ws_a.note_consumed(jg, pmg)
            em.wait("tensor", "ac", cu)
            psum_take(pu)
            pmu = em.mm_group(ps[pu], [(wbf(1, su_slot)[:, hc * 128:(hc + 1) * 128],
                                        xbf(hc)) for hc in range(HC)])
            ws_b.note_consumed(ju, pmu)
            em.wait("scalar", "pe", pmg)
            if v_a3_last is not None:
                em.wait("scalar", "dv", v_a3_last)
            em.op("scalar", lambda e, p=ps[pg]: e.activation(sg0[:, :], p, AF.Silu))
            a_sil = em.act_fence()
            ps_free[pg] = ("ac", a_sil)
            em.wait("vector", "pe", pmu)
            em.wait("vector", "ac", a_sil)
            if f >= 2:
                em.wait("vector", "sp%d" % (f % 2), spill[f - 2])
            em.op("vector",
                  lambda e, p=ps[pu], o=actout[f % 2]: e.tensor_mul(o, sg0[:, :], p))
            v_a = em.dve_fence()
            ps_free[pu] = ("dv", v_a)
            v_a3_last = v_a
            em.wait("gpsimd", "dv", v_a)
            spill[f] = em.dmag(act_dram[f], actout[f % 2], "sp%d" % (f % 2))
        pe_after_A3 = em.cnt["pe"]
        spill_last = spill[FFC - 1]
        spill_lane_last = "sp%d" % ((FFC - 1) % 2)

        if STOP_A == "A3":
            return _fin(nc, em)

        def build_tmp(m, pe_guard):
            dstp, srcp = (m - 1) % 2, (m - 2) % 2
            em.wait("vector", "pe", pe_guard)
            ms = None
            for s in range(SC):
                if m == 1:
                    em.op("vector", lambda e, s=s, dp=dstp:
                          e.tensor_mul(tmp(dp, s), dlt(s), upb(s)))
                else:
                    em.op("vector", lambda e, s=s, dp=dstp, sp=srcp:
                          e.tensor_mul(tmp(dp, s), tmp(sp, s), dlt(s)))
                ms = em.op("vector", lambda e, s=s, dp=dstp, m=m:
                           e.tensor_scalar(tmp(dp, s), tmp(dp, s),
                                           cons["rcol"][:, m - 1:m], None, ALU.mult),
                           sem="dv", inc=1)
            return ms

        def fold_keep(pe_guard):
            em.wait("scalar", "pe", pe_guard)
            em.wait("scalar", "dv", em.cnt["dv"])
            em.wait("vector", "pe", pe_guard)
            ms = None
            for s in range(SC):
                a_ab = em.op("scalar", lambda e, s=s:
                             e.activation(tmp(0, s), dlt(s), AF.Abs),
                             sem="ac", inc=1)
                em.wait("vector", "ac", a_ab)
                em.op("vector", lambda e, s=s:
                      e.tensor_scalar(tmp(0, s), tmp(0, s), 2.5, None, ALU.is_le))
                em.bubble()
                ms = em.op("vector", lambda e, s=s:
                           e.tensor_mul(dlt(s), dlt(s), tmp(0, s)),
                           sem="dv", inc=1)
            return ms

        tmp_dump_ms = {}

        def dump_tmp(m):
            if not DEBUG_A:
                return
            dp = (m - 1) % 2
            em.wait("gpsimd", "dv", tmp_ms[m])
            for s_ in range(SC):
                tmp_dump_ms[m] = em.dmag(dbg_tmp[m - 1, s_], tmp(dp, s_), "st")

        em.wait("vector", "dv", a_A2_dlt)
        for scu in range(SC):
            em.wait("vector", "ac", up_done[scu])
        tmp_ms = {1: build_tmp(1, pe_after_A3)}
        dump_tmp(1)

        ws_f = WStream(em, "wf", lambda s: fstage(s)[:, :], lambda s: fbf(s)[:, :],
                       init_dma_wait=("dv", v_a3_last),
                       init_cast_wait=(spill_lane_last, spill_last))
        groups = [[0, 1], [2, 3], [4, 5], [6]]
        acc_last = [None] * HC
        first_acc_guard = False
        prev_group_pm = pe_after_A3
        for gi, terms in enumerate(groups):
            for t_i in terms:
                if t_i >= 2 and t_i not in tmp_ms:
                    if t_i == 5:
                        fold_keep(prev_group_pm)
                    if DEBUG_A and t_i - 2 in tmp_dump_ms:
                        em.wait("vector", "st", tmp_dump_ms[t_i - 2])
                        em.wait("scalar", "st", tmp_dump_ms[t_i - 2])
                    tmp_ms[t_i] = build_tmp(t_i, prev_group_pm)
                    dump_tmp(t_i)
            for oc in range(HC):
                par = oc % 2
                pairs = []
                consumed = []
                for t_i in terms:
                    src = laT[oc] if t_i == 0 else fuseT[t_i - 1, oc]
                    jj, ss, cc = ws_f.job(src)
                    em.wait("tensor", "ac", cc)
                    w = fbf(ss)
                    if t_i == 0:
                        pairs += [(w[:, s * 128:(s + 1) * 128], upb(s))
                                  for s in range(SC)]
                    else:
                        pairs += [(w[:, s * 128:(s + 1) * 128], tmp((t_i - 1) % 2, s))
                                  for s in range(SC)]
                    consumed.append(jj)
                for t_i in terms:
                    if t_i >= 1:
                        em.wait("tensor", "dv", tmp_ms[t_i])
                psum_take(par)
                pm = em.mm_group(ps[par], pairs)
                for jj in consumed:
                    ws_f.note_consumed(jj, pm)
                if gi == 0:
                    em.wait("scalar", "pe", pm)
                    if not first_acc_guard:
                        em.wait("scalar", "pe", pe_after_A3)
                        first_acc_guard = True
                    em.op("scalar", lambda e, oc=oc, p=ps[par]: e.copy(acc(oc), p))
                    ev = em.act_fence()
                    ps_free[par] = ("ac", ev)
                else:
                    em.wait("vector", "pe", pm)
                    if gi == 1:
                        em.wait("vector", "ac", acc_last[oc])
                    ev = em.op("vector", lambda e, oc=oc, p=ps[par]:
                               e.tensor_add(acc(oc), acc(oc), p),
                               sem="dv", inc=1)
                    ps_free[par] = ("dv", ev)
                acc_last[oc] = ev
            prev_group_pm = em.cnt["pe"]
        pe_after_A4 = em.cnt["pe"]

        if DEBUG_A:
            for oc_ in range(HC):
                em.wait("gpsimd", "dv", acc_last[oc_])
                em.dmag(dbg_acc4[oc_], acc(oc_), "st")
            dbg_gate4 = em.cnt["st"]
            em.wait("vector", "st", dbg_gate4)

        if STOP_A == "A4":
            return _fin(nc, em)

        ws_d = WStream(em, "wd", lambda s: dstage(s)[:, :], lambda s: dbf(s)[:, :],
                       init_dma_wait=("pe", pe_after_A4),
                       init_cast_wait=("pe", pe_after_A4))
        actg_ap = ar.b(XZ, GF * T)
        prev_group_pe = pe_after_A4
        for g in range(NG):
            em.wait("sync", "pe", prev_group_pe)
            hi = (g + 1) * GF
            n_even = (hi + 1) // 2
            n_odd = hi // 2
            em.wait("sync", "sp0", 16 * n_even)
            em.wait("sync", "sp1", 16 * n_odd)
            dl = em.dmas(actg_ap.rearrange("p (f t) -> p f t", f=GF),
                         act_dram[g * GF:(g + 1) * GF].rearrange("f p t -> p f t"),
                         "ag")
            em.wait("tensor", "ag", dl)
            pm = None
            for oc in range(HC):
                par = oc % 2
                jj, ss, cc = ws_d.job(dwT[oc][:, g * GF * 128:(g + 1) * GF * 128])
                em.wait("tensor", "ac", cc)
                psum_take(par)
                w = dbf(ss)
                pairs = [(w[:, i * 128:(i + 1) * 128], actg(i)) for i in range(GF)]
                pm = em.mm_group(ps[par], pairs)
                ws_d.note_consumed(jj, pm)
                em.wait("vector", "pe", pm)
                ev = em.op("vector", lambda e, oc=oc, p=ps[par]:
                           e.tensor_add(acc(oc), acc(oc), p),
                           sem="dv", inc=1)
                ps_free[par] = ("dv", ev)
                acc_last[oc] = ev
            prev_group_pe = pm

        if DEBUG_A:
            for oc_ in range(HC):
                em.wait("gpsimd", "dv", acc_last[oc_])
                em.dmag(dbg_acc5[oc_], acc(oc_), "st")
            dbg_gate5 = em.cnt["st"]
            em.wait("vector", "st", dbg_gate5)

        if STOP_A == "A5":
            return _fin(nc, em)

        xin_sl = [ar.f(MA, T), ar.f(MB, T)]
        add_ms = [None] * HC
        last_cast = em.cnt["ac"]
        for oc in range(HC):
            lane = "xi%d" % (oc % 2)
            if oc >= 2:
                em.wait("sync", "dv", add_ms[oc - 2])
            else:
                em.wait("sync", "ac", last_cast)
            dxi = em.dmas(xin_sl[oc % 2][:, :], xT[oc], lane)
            em.wait("vector", lane, dxi)
            em.wait("vector", "dv", acc_last[oc])
            add_ms[oc] = em.op("vector", lambda e, oc=oc:
                               e.tensor_add(acc(oc), acc(oc), xin_sl[oc % 2][:, :]),
                               sem="dv", inc=1)
        em.wait("scalar", "dv", add_ms[HC - 1])
        psum_take(0)
        psum_take(1)
        psum_take(2)
        em.wait("vector", "pe", prev_group_pe)
        ln1 = _emit_ln(em, ar, cons, ps[0], ps[1], ps[2],
                       src_bf=None, src_fp=lambda oc: acc(oc),
                       dst_fn=lambda oc: (x2bf(oc), acc(oc)))
        ps_free[0] = ("dv", ln1)
        ps_free[1] = ("dv", ln1)
        ps_free[2] = ("dv", ln1)

        if DEBUG_A:
            em.wait("gpsimd", "dv", ln1)
            for oc_ in range(HC):
                em.dmag(dbg_x2[oc_], x2bf(oc_), "st")
            dbg_gate2 = em.cnt["st"]
            em.wait("scalar", "st", dbg_gate2)

        if STOP_A == "A6":
            return _fin(nc, em)

        em.wait("sync", "dv", ln1)
        d_rw = em.dmas(rw_st[:, :], rwT_d[:, :], "cst")
        em.wait("scalar", "cst", d_rw)
        em.wait("scalar", "dv", ln1)
        a_rw = em.op("scalar", lambda e: e.copy(rw_bf[:, :], rw_st[:, :]),
                     sem="ac", inc=1)
        em.wait("tensor", "ac", a_rw)
        em.wait("tensor", "dv", ln1)
        cb_ms = [None] * 8
        prev_a_ab = 0
        prev_pt = 0
        for tt in range(8):
            psum_take(2)
            pairs = [(x2bf(hc)[:, tt * 128:(tt + 1) * 128],
                      rw_bf[:, hc * 16:(hc + 1) * 16]) for hc in range(HC)]
            pm = em.mm_group(ps[2][:, 0:16], pairs, nsplit=1)
            em.wait("vector", "pe", pm)
            em.wait("vector", "ac", prev_a_ab)
            em.op("vector", lambda e: e.tensor_add(cons["lg"][:, :],
                                                   ps[2][:, 0:16],
                                                   cons["srowb"][:, :]))
            _pairwise(em, ALU.max, cons["lg"], 16, cons["red"], cons["mxc"])
            em.bubble()
            em.op("vector", lambda e: e.tensor_scalar(
                cons["nmax"][:, :], cons["mxc"][:, :], -1.0, None, ALU.mult))
            v_lg = em.dve_fence()
            em.wait("scalar", "dv", v_lg)
            em.op("scalar", lambda e: e.activation(cons["lg"][:, :],
                                                   cons["lg"][:, :], AF.Exp,
                                                   bias=cons["nmax"][:, 0:1]))
            a_ex = em.act_fence()
            em.wait("vector", "ac", a_ex)
            em.op("vector", lambda e: e.tensor_mul(cons["emx"][:, :], cons["lg"][:, :],
                                                   cons["mrowb"][:, :]))
            _pairwise(em, ALU.add, cons["emx"], 16, cons["red"], cons["sums"])
            v_sm = em.dve_fence()
            em.wait("scalar", "dv", v_sm)
            em.op("scalar", lambda e: e.activation(cons["rcp"][:, :],
                                                   cons["sums"][:, :], AF.Ln))
            em.op("scalar", lambda e: e.activation(cons["rcp"][:, :],
                                                   cons["rcp"][:, :], AF.Exp,
                                                   scale=-1.0))
            a_rc = em.act_fence()
            em.wait("vector", "ac", a_rc)
            em.op("vector", lambda e: e.tensor_scalar(
                cons["emx"][:, :], cons["emx"][:, :], cons["rcp"][:, 0:1], 2.0,
                ALU.mult, ALU.mult))
            v_al = em.dve_fence()
            em.wait("scalar", "dv", v_al)
            em.wait("scalar", "pe", prev_pt)
            em.op("scalar", lambda e: e.copy(cons["abf"][:, :], cons["emx"][:, :]))
            a_ab = em.act_fence()
            prev_a_ab = a_ab
            em.wait("tensor", "ac", a_ab)
            psum_take("tA")
            pt_ = em.op("tensor", lambda e:
                        e.transpose(pst[0][0:16, 0:128], cons["abf"][:, :],
                                    cons["identb"][:, :]),
                        sem="pe", inc=1)
            prev_pt = pt_
            em.wait("vector", "pe", pt_)
            v_at = em.op("vector", lambda e:
                         e.tensor_single_scalar(cons["at"][:, :],
                                                pst[0][0:16, 0:128], 1.0, ALU.mult),
                         sem="dv", inc=1)
            ps_free["tA"] = ("dv", v_at)
            if DEBUG_A and tt == 0:
                em.wait("gpsimd", "dv", v_at)
                em.dmag(dbg_rt[0][:, 0:16], cons["lg"][:, :], "st")
                em.dmag(dbg_rt[1][:, 0:16], cons["emx"][:, :], "st")
                em.dmag(dbg_rt[2][0:16, :], cons["at"][:, :], "st")
                em.dmag(dbg_rt[3][:, 0:16], cons["srowb"][:, :], "st")
                em.dmag(dbg_rt[4][:, 0:16], cons["mrowb"][:, :], "st")
                dsnap = em.dmag(dbg_rt[5][:, 0:16], cons["abf"][:, :], "st")
                em.wait("scalar", "st", dsnap)
                em.wait("vector", "st", dsnap)
            em.wait("tensor", "dv", v_at)
            pb = em.op("tensor", lambda e:
                       e.matmul(ps[2][:, 128:256], cons["m16"][:, :], cons["at"][:, :],
                                start=True, stop=True),
                       sem="pe", inc=1)
            em.wait("vector", "pe", pb)
            cb_ms[tt] = em.op("vector", lambda e, tt=tt:
                              e.tensor_single_scalar(cbx[:, tt * 128:(tt + 1) * 128],
                                                     ps[2][:, 128:256], 1.0, ALU.mult),
                              sem="dv", inc=1)
            ps_free[2] = ("dv", cb_ms[tt])

        ws7 = WStream(em, "w7", lambda s: wstage7(s)[:, :], lambda s: wbf7(s)[:, :],
                      init_dma_wait=("dv", ln1), init_cast_wait=("dv", ln1))
        jz, sz, cz = ws7.job(AaT[0])
        em.wait("tensor", "ac", cz)
        psum_take(0)
        pairs = [(wbf7(sz)[:, hc * 128:(hc + 1) * 128], x2bf(hc)) for hc in range(HC)]
        pmz = em.mm_group(ps[0], pairs)
        ws7.note_consumed(jz, pmz)
        em.wait("vector", "pe", pmz)
        em.wait("vector", "dv", cb_ms[7])
        em.op("vector", lambda e: e.tensor_mul(hw[:, :], cbx[:, :], ps[0][:, :]))
        v_hw = em.dve_fence()
        ps_free[0] = ("dv", v_hw)
        if DEBUG_A:
            em.wait("gpsimd", "dv", v_hw)
            em.dmag(dbg_cb[:, :], cbx[:, :], "st")
            dhw = em.dmag(dbg_hw[:, :], hw[:, :], "st")
            em.wait("vector", "st", dhw)
            em.wait("scalar", "st", dhw)
            em.wait("sync", "st", dhw)

        ws_ba = WStream(em, "wx", lambda s: ba_st(s)[:, :], lambda s: ba_bf(s)[:, :],
                        init_dma_wait=("dv", ln1), init_cast_wait=("dv", ln1))
        vbf_ms = [None] * HC
        first_vbf_guard = False
        for oc in range(HC):
            par = oc % 2
            jw, sw, cw = ws7.job(wbT[oc])
            jb, sb, cb2 = ws_ba.job(BaT[oc])
            em.wait("tensor", "ac", cw)
            em.wait("tensor", "ac", cb2)
            em.wait("tensor", "dv", v_hw)
            psum_take(par)
            pairs = [(wbf7(sw)[:, hc * 128:(hc + 1) * 128], x2bf(hc))
                     for hc in range(HC)]
            pairs.append((ba_bf(sb)[:, :], hw[:, :]))
            pm = em.mm_group(ps[par], pairs)
            ws7.note_consumed(jw, pm)
            ws_ba.note_consumed(jb, pm)
            em.wait("vector", "pe", pm)
            if not first_vbf_guard:
                em.wait("vector", "dv", ln1)
                first_vbf_guard = True
            vbf_ms[oc] = em.op("vector", lambda e, oc=oc, p=ps[par]:
                               e.scalar_tensor_tensor(vbf(oc), p,
                                                      cons["bb"][:, oc:oc + 1],
                                                      x2bf(oc), ALU.add, ALU.add),
                               sem="dv", inc=1)
            ps_free[par] = ("dv", vbf_ms[oc])
        if DEBUG_A:
            for oc_ in range(HC):
                em.wait("gpsimd", "dv", vbf_ms[oc_])
                em.dmag(dbg_vbf[oc_], vbf(oc_), "st")
            dbg_gatev = em.cnt["st"]
            em.wait("vector", "st", dbg_gatev)
        em.wait("scalar", "dv", vbf_ms[HC - 1])
        psum_take(0)
        psum_take(1)
        psum_take(2)
        ln2 = _emit_ln(em, ar, cons, ps[0], ps[1], ps[2],
                       src_bf=lambda oc: vbf(oc), src_fp=None,
                       dst_fn=lambda oc: (vbf(oc), vbf(oc)))
        ps_free[0] = ("dv", ln2)
        ps_free[1] = ("dv", ln2)
        ps_free[2] = ("dv", ln2)
        x3 = vbf

        if STOP_A == "A7":
            return _fin(nc, em)

        qkv_ms = [None] * 24
        first_q_guard = False
        for oc in range(24):
            par = oc % 2
            jw, sw, cw = ws7.job(wqkvT[oc])
            em.wait("tensor", "ac", cw)
            em.wait("tensor", "dv", ln2)
            psum_take(par)
            pairs = [(wbf7(sw)[:, hc * 128:(hc + 1) * 128], x3(hc))
                     for hc in range(HC)]
            pm = em.mm_group(ps[par], pairs)
            ws7.note_consumed(jw, pm)
            em.wait("scalar", "pe", pm)
            if not first_q_guard:
                em.wait("scalar", "dv", ln2)
                first_q_guard = True
            if oc < 8:
                em.op("scalar", lambda e, oc=oc, p=ps[par]:
                      e.activation(qch(oc), p, AF.Identity,
                                   bias=cons["bq32"][:, oc:oc + 1],
                                   scale=1.0 / 32.0))
            elif oc < 16:
                em.op("scalar", lambda e, oc=oc, p=ps[par]:
                      e.activation(kch(oc - 8), p, AF.Identity,
                                   bias=cons["bqkv"][:, oc:oc + 1]))
            else:
                em.op("scalar", lambda e, oc=oc, p=ps[par]:
                      e.activation(vch(oc - 16), p, AF.Identity,
                                   bias=cons["bqkv"][:, oc:oc + 1]))
            ev = em.act_fence()
            qkv_ms[oc] = ev
            ps_free[par] = ("ac", ev)

        tr_ms = None
        for dc in range(DC):
            em.wait("tensor", "ac", qkv_ms[16 + dc])
            for tc in range(8):
                bank = "tA" if (dc * 8 + tc) % 2 == 0 else "tB"
                pp = pst[0] if bank == "tA" else pst[1]
                psum_take(bank)
                pt_ = em.op("tensor", lambda e, dc=dc, tc=tc, pp=pp:
                            e.transpose(pp[:, 0:128],
                                        vch(dc)[:, tc * 128:(tc + 1) * 128],
                                        cons["identb"][:, :]),
                            sem="pe", inc=1)
                em.wait("vector", "pe", pt_)
                tr_ms = em.op("vector", lambda e, dc=dc, tc=tc, pp=pp:
                              e.tensor_single_scalar(vtch(tc)[:, dc * 128:(dc + 1) * 128],
                                                     pp[:, 0:128], 1.0, ALU.mult),
                              sem="dv", inc=1)
                ps_free[bank] = ("dv", tr_ms)

        tr_ms = em.dve_fence()
        em.wait("gpsimd", "dv", ln2)
        for hc in range(HC):
            em.dmag(x3o[hc], x3(hc), "st")
        em.wait("gpsimd", "ac", qkv_ms[15])
        for i in range(DC):
            em.dmag(qo[i], qch(i), "st")
            em.dmag(ko[i], kch(i), "st")
        em.wait("gpsimd", "dv", tr_ms)
        last_st = 0
        for i in range(DC):
            last_st = em.dmag(vto[i], vtch(i), "st")
        em.wait("gpsimd", "st", last_st)
        em.wait("gpsimd", "sp0", em.cnt["sp0"])
        em.wait("gpsimd", "sp1", em.cnt["sp1"])

        sems = {}
        with ExitStack() as sctx:
            for sname in sorted(em.cnt.keys()):
                sems[sname] = sctx.enter_context(nc.semaphore(sname))
            block = sctx.enter_context(nc.Block())
            em.finalize(block, sems)

    return nc



def build_B():
    nc = bass.Bass("TRN2", target_bir_lowering=False,
                   detect_race_conditions=False)

    q_d = nc.dram_tensor("q", [DC, 128, T], BF16, kind="ExternalInput").ap()
    K_d = nc.dram_tensor("K", [DC, 128, 4 * T], BF16, kind="ExternalInput").ap()
    VT_d = nc.dram_tensor("VT", [DC, 128, 32, 128], BF16, kind="ExternalInput").ap()
    x3_d = nc.dram_tensor("x3", [HC, 128, T], BF16, kind="ExternalInput").ap()
    woT = nc.dram_tensor("woT", [HC, 128, D], F32, kind="ExternalInput").ap()
    bo_d = nc.dram_tensor("bo", [128, HC], F32, kind="ExternalInput").ap()
    wb1T = nc.dram_tensor("wb1T", [HC, 128, H], F32, kind="ExternalInput").ap()
    bb1_d = nc.dram_tensor("bb1", [128, HC], F32, kind="ExternalInput").ap()
    A1T = nc.dram_tensor("A1T", [1, 128, H], F32, kind="ExternalInput").ap()
    B1T = nc.dram_tensor("B1T", [HC, 128, 128], F32, kind="ExternalInput").ap()
    rw1_d = nc.dram_tensor("rw1", [128, 256], F32, kind="ExternalInput").ap()
    m8_d = nc.dram_tensor("m8", [8, 128], BF16, kind="ExternalInput").ap()
    idb_d = nc.dram_tensor("identb", [128, 128], BF16, kind="ExternalInput").ap()

    out_d = nc.dram_tensor("out", [HC, 128, T], F32, kind="ExternalOutput").ap()
    if DEBUG_B:
        dbg_x4 = nc.dram_tensor("dbg_x4", [HC, 128, T], BF16, kind="ExternalOutput").ap()
        dbg_ao = nc.dram_tensor("dbg_ao", [DC, 128, T], BF16, kind="ExternalOutput").ap()
        dbg_sc = nc.dram_tensor("dbg_sc", [128, 4 * T], F32, kind="ExternalOutput").ap()
        dbg_pt = nc.dram_tensor("dbg_pt", [2, 128, T], BF16, kind="ExternalOutput").ap()
        dbg_vbfB = nc.dram_tensor("dbg_vbfB", [HC, 128, T], BF16, kind="ExternalOutput").ap()
        dbg_cbB = nc.dram_tensor("dbg_cbB", [128, T], F32, kind="ExternalOutput").ap()

    with ExitStack() as ctx:
        ar = Arena(nc, ARENA_BYTES)
        ps = [ctx.enter_context(nc.psum_tensor(f"ps{i}", [128, 1024], F32)).ap()
              for i in range(3)]
        pst = [ctx.enter_context(nc.psum_tensor(f"pst{i}", [128, 1024], BF16)).ap()
               for i in range(2)]

        cons = _consts(nc, [
            ("bo", (128, HC), F32), ("bb1", (128, HC), F32),
            ("m8", (8, 128), BF16), ("identb", (128, 128), BF16),
            ("ones1", (1, 128), F32), ("onesb", (128, 1), BF16),
            ("cols", (128, 8), F32), ("red", (128, 8), F32),
            ("lg", (128, 8), F32), ("msk", (128, 8), F32), ("emx", (128, 8), F32),
            ("abf", (128, 8), BF16), ("at", (8, 128), BF16),
            ("epsc", (1, 2), F32), ("fence", (1, 256), BF16),
        ])
        cons["nmax"] = cons["cols"][:, 0:1]
        cons["sums"] = cons["cols"][:, 1:2]
        cons["rcp"] = cons["cols"][:, 2:3]
        cons["mxc"] = cons["cols"][:, 3:4]
        cons["m1"] = cons["cols"][:, 4:5]
        cons["epsln"] = cons["epsc"][0:1, 1:2]

        em = Em(nc)
        em.set_fence_cell(cons["fence"][0:1, 0:256], cons["fence"][0:1, 0:256])

        PT = lambda kc: ar.b(XA + kc * 2048, T)
        Kc = lambda dc: ar.b(XB + dc * 8192, 4 * T)
        acc = lambda oc: ar.f(XA + oc * 4096, T)
        vbf = lambda oc: ar.b(XB + oc * 2048, T)
        qc = lambda i: ar.b(XZ + i * 2048, T)
        scores = ar.f(XZ + 16 * KB, 4 * T)
        pbf = ar.b(XZ + 40 * KB, 4 * T)
        aout = lambda dc: ar.b(XZ + 32 * KB + dc * 2048, T)
        vt_st = lambda s: ar.b(XZ + 48 * KB + s * 8 * KB, 32 * 128)
        x4 = lambda oc: ar.b(XZ + oc * 2048, T)
        wstage = lambda s: ar.f(XA + s * 16 * KB, H)
        wbf = lambda s: ar.b(XA + 32 * KB + s * 8 * KB, H)
        ba_st = lambda s: ar.f(XA + 48 * KB + s * KB, 128)
        ba_bf = lambda s: ar.b(XA + 50 * KB + s * 512, 128)
        rw_st = ar.f(XA + 52 * KB, 256)
        rw_bf = ar.b(XA + 53 * KB, 256)
        outst = [ar.f(XA + 54 * KB, T), ar.f(XA + 58 * KB, T)]
        hw = ar.b(MC, T)
        cbx = ar.f(MB, T)
        scr_att = ar.f(MA, 2048)

        ps_free = {0: ("pe", 0), 1: ("pe", 0), 2: ("pe", 0),
                   "tA": ("pe", 0), "tB": ("pe", 0)}

        def psum_take(par):
            sname, sval = ps_free[par]
            em.wait("tensor", sname, sval)

        em.dmas(cons["bo"][:, :], bo_d[:, :], "cst")
        em.dmas(cons["bb1"][:, :], bb1_d[:, :], "cst")
        em.dmas(cons["m8"][:, :], m8_d[:, :], "cst")
        d_c = em.dmas(cons["identb"][:, :], idb_d[:, :], "cst")
        em.wait("vector", "cst", d_c)
        em.op("vector", lambda e: e.memset(cons["ones1"][:, :], 1.0))
        em.op("vector", lambda e: e.memset(cons["epsln"][:, :], LN_EPS))
        em.op("vector", lambda e: e.memset(cons["fence"][:, :], 1.0))
        v_c = em.op("vector", lambda e: e.memset(cons["onesb"][:, :], 1.0),
                    sem="dv", inc=1)
        em.wait("scalar", "dv", v_c)

        em.wait("tensor", "dv", v_c)
        for i in range(DC):
            em.dmas(qc(i), q_d[i], "qk")
        qk_total = 0
        for dc in range(DC):
            qk_total = em.dmas(Kc(dc), K_d[dc], "qk")

        pt_ms = [None] * 32
        prev_v_pb = 0
        for qt in range(8):
            em.wait("tensor", "qk", qk_total)
            first_kb = True
            for kb in range(4):
                par = kb % 2
                psum_take(par)
                pairs = [(qc(dc)[:, qt * 128:(qt + 1) * 128],
                          Kc(dc)[:, kb * T:(kb + 1) * T]) for dc in range(DC)]
                pm = em.mm_group(ps[par], pairs)
                em.wait("scalar", "pe", pm)
                if first_kb:
                    em.wait("scalar", "dv", prev_v_pb)
                    first_kb = False
                em.op("scalar", lambda e, kb=kb, p=ps[par]:
                      e.copy(scores[:, kb * T:(kb + 1) * T], p))
                ev = em.act_fence()
                ps_free[par] = ("ac", ev)
            em.wait("vector", "ac", em.cnt["ac"])
            _pairwise(em, ALU.max, scores, 4096, scr_att, cons["mxc"])
            em.bubble()
            em.op("vector", lambda e: e.tensor_scalar(
                cons["nmax"][:, :], cons["mxc"][:, :], -1.0, None, ALU.mult))
            v_mx = em.dve_fence()
            em.wait("scalar", "dv", v_mx)
            em.op("scalar", lambda e: e.activation(
                scores[:, :], scores[:, :], AF.Exp, bias=cons["nmax"][:, 0:1]))
            a_ex = em.act_fence()
            em.wait("vector", "ac", a_ex)
            _pairwise(em, ALU.add, scores, 4096, scr_att, cons["sums"])
            v_sm = em.dve_fence()
            em.wait("scalar", "dv", v_sm)
            em.op("scalar", lambda e: e.activation(cons["rcp"][:, :],
                                                   cons["sums"][:, :], AF.Ln))
            em.op("scalar", lambda e: e.activation(cons["rcp"][:, :],
                                                   cons["rcp"][:, :], AF.Exp,
                                                   scale=-1.0))
            a_rc = em.act_fence()
            em.wait("vector", "ac", a_rc)
            em.op("vector", lambda e: e.tensor_scalar(
                pbf[:, :], scores[:, :], cons["rcp"][:, 0:1], None, ALU.mult))
            v_pb = em.dve_fence()
            prev_v_pb = v_pb
            if DEBUG_B and qt == 0:
                em.wait("gpsimd", "dv", v_pb)
                dsc = em.dmag(dbg_sc[:, :], pbf[:, :], "st0")
                em.wait("scalar", "st0", dsc)
            em.wait("tensor", "dv", v_pb)
            for kc in range(32):
                bank = "tA" if kc % 2 == 0 else "tB"
                pp = pst[0] if kc % 2 == 0 else pst[1]
                psum_take(bank)
                pt_ = em.op("tensor", lambda e, kc=kc, pp=pp:
                            e.transpose(pp[:, 0:128],
                                        pbf[:, kc * 128:(kc + 1) * 128],
                                        cons["identb"][:, :]),
                            sem="pe", inc=1)
                em.wait("vector", "pe", pt_)
                ms = em.op("vector", lambda e, kc=kc, qt=qt, pp=pp:
                           e.tensor_single_scalar(PT(kc)[:, qt * 128:(qt + 1) * 128],
                                                  pp[:, 0:128], 1.0, ALU.mult),
                           sem="dv", inc=1)
                ps_free[bank] = ("dv", ms)
                pt_ms[kc] = ms
            pt_fence = em.dve_fence()
            for kc in range(32):
                pt_ms[kc] = pt_fence

        if DEBUG_B:
            em.wait("gpsimd", "dv", pt_ms[31])
            em.dmag(dbg_pt[0], PT(0), "st0")
            dpt = em.dmag(dbg_pt[1], PT(31), "st0")
            em.wait("tensor", "st0", dpt)
        pe_after_scores = em.cnt["pe"]

        vt_cast = {}
        ao_ms = [None] * DC
        for dc in range(DC):
            slot = dc % 2
            lane = "vt%d" % slot
            if dc >= 2:
                em.wait("sync", "pe", vt_cast[dc - 2])
            dvt = em.dmas(vt_st(slot).rearrange("p (kc c) -> p kc c", kc=32),
                          VT_d[dc], lane)
            em.wait("tensor", lane, dvt)
            for kc in range(32):
                em.wait("tensor", "dv", pt_ms[kc]) if dc == 0 else None
            par = dc % 2
            psum_take(par)
            pairs = [(vt_st(slot)[:, kc * 128:(kc + 1) * 128], PT(kc))
                     for kc in range(32)]
            pm = em.mm_group(ps[par], pairs)
            vt_cast[dc] = pm
            em.wait("scalar", "pe", pm)
            em.op("scalar", lambda e, dc=dc, p=ps[par]: e.copy(aout(dc), p))
            ao_ms[dc] = em.act_fence()
            ps_free[par] = ("ac", ao_ms[dc])
        if DEBUG_B:
            em.wait("gpsimd", "ac", ao_ms[DC - 1])
            for dc_ in range(DC):
                dao = em.dmag(dbg_ao[dc_], aout(dc_), "st0")
            em.wait("vector", "st0", dao)

        wo_st = lambda s: ar.f(XZ + s * 4 * KB, D)
        wo_bf = lambda s: ar.b(XZ + 8 * KB + s * 2 * KB, D)
        ws_o = WStream(em, "wo", lambda s: wo_st(s)[:, :], lambda s: wo_bf(s)[:, :],
                       init_dma_wait=("pe", pe_after_scores),
                       init_cast_wait=("pe", pe_after_scores))
        x3_slb = [ar.b(MA, T), ar.b(MA + 2 * KB, T)]
        acc_ms = [None] * HC
        first_acc = False
        for oc in range(HC):
            par = oc % 2
            jw, sw, cw = ws_o.job(woT[oc][:, :])
            em.wait("tensor", "ac", cw)
            em.wait("tensor", "ac", ao_ms[DC - 1])
            psum_take(par)
            pairs = [(wo_bf(sw)[:, dc * 128:(dc + 1) * 128], aout(dc))
                     for dc in range(DC)]
            pm = em.mm_group(ps[par], pairs)
            ws_o.note_consumed(jw, pm)
            lane = "x3%d" % (oc % 2)
            if oc >= 2:
                em.wait("sync", "dv", acc_ms[oc - 2])
            dx3 = em.dmas(x3_slb[oc % 2][:, :], x3_d[oc], lane)
            em.wait("vector", lane, dx3)
            em.wait("vector", "pe", pm)
            if not first_acc:
                first_acc = True
            acc_ms[oc] = em.op("vector", lambda e, oc=oc, p=ps[par]:
                               e.scalar_tensor_tensor(acc(oc), p,
                                                      cons["bo"][:, oc:oc + 1],
                                                      x3_slb[oc % 2][:, :],
                                                      ALU.add, ALU.add),
                               sem="dv", inc=1)
            ps_free[par] = ("dv", acc_ms[oc])
        em.wait("scalar", "dv", acc_ms[HC - 1])
        psum_take(0)
        psum_take(1)
        psum_take(2)
        ln3 = _emit_ln(em, ar, cons, ps[0], ps[1], ps[2],
                       src_bf=None, src_fp=lambda oc: acc(oc),
                       dst_fn=lambda oc: (x4(oc), acc(oc)))
        ps_free[0] = ("dv", ln3)
        ps_free[1] = ("dv", ln3)
        ps_free[2] = ("dv", ln3)
        if DEBUG_B:
            em.wait("gpsimd", "dv", ln3)
            for oc_ in range(HC):
                dx4 = em.dmag(dbg_x4[oc_], x4(oc_), "st0")
            em.wait("vector", "st0", dx4)
            em.wait("scalar", "st0", dx4)

        em.wait("sync", "dv", ln3)
        d_rw = em.dmas(rw_st[:, :], rw1_d[:, :], "cst")
        em.wait("scalar", "cst", d_rw)
        em.wait("scalar", "dv", ln3)
        a_rw = em.op("scalar", lambda e: e.copy(rw_bf[:, :], rw_st[:, :]),
                     sem="ac", inc=1)
        em.wait("tensor", "ac", a_rw)
        em.wait("tensor", "dv", ln3)
        cb_ms = [None] * 8
        prev_a_ab = 0
        prev_pt = 0
        for tt in range(8):
            psum_take(2)
            pairs = [(x4(hc)[:, tt * 128:(tt + 1) * 128],
                      rw_bf[:, hc * 8:(hc + 1) * 8]) for hc in range(HC)]
            pm = em.mm_group(ps[2][:, 0:8], pairs, nsplit=1)
            em.wait("vector", "pe", pm)
            em.wait("vector", "ac", prev_a_ab)
            em.op("vector", lambda e: e.tensor_single_scalar(
                cons["lg"][:, :], ps[2][:, 0:8], 1.0, ALU.mult))
            _pairwise(em, ALU.max, cons["lg"], 8, cons["red"], cons["m1"])
            em.bubble()
            em.op("vector", lambda e: e.tensor_scalar(
                cons["nmax"][:, :], cons["m1"][:, :], -1.0, None, ALU.mult))
            em.op("vector", lambda e: e.tensor_scalar(cons["msk"][:, :],
                                                      cons["lg"][:, :],
                                                      cons["m1"][:, 0:1], None,
                                                      ALU.is_ge))
            em.bubble()
            em.op("vector", lambda e: e.scalar_tensor_tensor(
                cons["msk"][:, :], cons["msk"][:, :], -1e30, cons["lg"][:, :],
                ALU.mult, ALU.add))
            _pairwise(em, ALU.max, cons["msk"], 8, cons["red"], cons["m1"])
            em.bubble()
            em.op("vector", lambda e: e.tensor_scalar(cons["msk"][:, :],
                                                      cons["lg"][:, :],
                                                      cons["m1"][:, 0:1], None,
                                                      ALU.is_ge))
            v_m2 = em.dve_fence()
            em.wait("scalar", "dv", v_m2)
            em.op("scalar", lambda e: e.activation(cons["lg"][:, :],
                                                   cons["lg"][:, :], AF.Exp,
                                                   bias=cons["nmax"][:, 0:1]))
            a_ex = em.act_fence()
            em.wait("vector", "ac", a_ex)
            em.op("vector", lambda e: e.tensor_mul(cons["emx"][:, :], cons["lg"][:, :],
                                                   cons["msk"][:, :]))
            _pairwise(em, ALU.add, cons["emx"], 8, cons["red"], cons["sums"])
            v_sm = em.dve_fence()
            em.wait("scalar", "dv", v_sm)
            em.op("scalar", lambda e: e.activation(cons["rcp"][:, :],
                                                   cons["sums"][:, :], AF.Ln))
            em.op("scalar", lambda e: e.activation(cons["rcp"][:, :],
                                                   cons["rcp"][:, :], AF.Exp,
                                                   scale=-1.0))
            a_rc = em.act_fence()
            em.wait("vector", "ac", a_rc)
            em.op("vector", lambda e: e.tensor_scalar(
                cons["emx"][:, :], cons["emx"][:, :], cons["rcp"][:, 0:1], 2.0,
                ALU.mult, ALU.mult))
            v_al = em.dve_fence()
            em.wait("scalar", "dv", v_al)
            em.wait("scalar", "pe", prev_pt)
            em.op("scalar", lambda e: e.copy(cons["abf"][:, :], cons["emx"][:, :]))
            a_ab = em.act_fence()
            prev_a_ab = a_ab
            em.wait("tensor", "ac", a_ab)
            psum_take("tA")
            pt_ = em.op("tensor", lambda e:
                        e.transpose(pst[0][0:8, 0:128], cons["abf"][:, :],
                                    cons["identb"][:, :]),
                        sem="pe", inc=1)
            prev_pt = pt_
            em.wait("vector", "pe", pt_)
            em.op("vector", lambda e:
                  e.tensor_single_scalar(cons["at"][:, :],
                                         pst[0][0:8, 0:128], 1.0, ALU.mult))
            v_at = em.dve_fence()
            ps_free["tA"] = ("dv", v_at)
            em.wait("tensor", "dv", v_at)
            pb = em.op("tensor", lambda e:
                       e.matmul(ps[2][:, 128:256], cons["m8"][:, :], cons["at"][:, :],
                                start=True, stop=True),
                       sem="pe", inc=1)
            em.wait("vector", "pe", pb)
            cb_ms[tt] = em.op("vector", lambda e, tt=tt:
                              e.tensor_single_scalar(cbx[:, tt * 128:(tt + 1) * 128],
                                                     ps[2][:, 128:256], 1.0, ALU.mult),
                              sem="dv", inc=1)
            ps_free[2] = ("dv", cb_ms[tt])

        ws_w = WStream(em, "ww", lambda s: wstage(s)[:, :], lambda s: wbf(s)[:, :],
                       init_dma_wait=("dv", ln3), init_cast_wait=("dv", ln3))
        ws_b1 = WStream(em, "wc", lambda s: ba_st(s)[:, :], lambda s: ba_bf(s)[:, :],
                        init_dma_wait=("dv", ln3), init_cast_wait=("dv", ln3))
        jz, sz, cz = ws_w.job(A1T[0])
        em.wait("tensor", "ac", cz)
        psum_take(0)
        pairs = [(wbf(sz)[:, hc * 128:(hc + 1) * 128], x4(hc)) for hc in range(HC)]
        pmz = em.mm_group(ps[0], pairs)
        ws_w.note_consumed(jz, pmz)
        em.wait("vector", "pe", pmz)
        em.wait("vector", "dv", cb_ms[7])
        em.op("vector", lambda e: e.tensor_mul(hw[:, :], cbx[:, :], ps[0][:, :]))
        v_hw = em.dve_fence()
        ps_free[0] = ("dv", v_hw)
        if DEBUG_B:
            em.wait("gpsimd", "dv", v_hw)
            dcb = em.dmag(dbg_cbB[:, :], cbx[:, :], "st0")
            em.wait("vector", "st0", dcb)
            em.wait("scalar", "st0", dcb)

        vbf_ms = [None] * HC
        first_v = False
        for oc in range(HC):
            par = oc % 2
            jw, sw, cw = ws_w.job(wb1T[oc])
            jb, sb, cb2 = ws_b1.job(B1T[oc])
            em.wait("tensor", "ac", cw)
            em.wait("tensor", "ac", cb2)
            em.wait("tensor", "dv", v_hw)
            psum_take(par)
            pairs = [(wbf(sw)[:, hc * 128:(hc + 1) * 128], x4(hc))
                     for hc in range(HC)]
            pairs.append((ba_bf(sb)[:, :], hw[:, :]))
            pm = em.mm_group(ps[par], pairs)
            ws_w.note_consumed(jw, pm)
            ws_b1.note_consumed(jb, pm)
            em.wait("vector", "pe", pm)
            if not first_v:
                em.wait("vector", "dv", ln3)
                first_v = True
            vbf_ms[oc] = em.op("vector", lambda e, oc=oc, p=ps[par]:
                               e.scalar_tensor_tensor(vbf(oc), p,
                                                      cons["bb1"][:, oc:oc + 1],
                                                      x4(oc), ALU.add, ALU.add),
                               sem="dv", inc=1)
            ps_free[par] = ("dv", vbf_ms[oc])
        if DEBUG_B:
            em.wait("gpsimd", "dv", vbf_ms[HC - 1])
            for oc_ in range(HC):
                dvb = em.dmag(dbg_vbfB[oc_], vbf(oc_), "st0")
            em.wait("vector", "st0", dvb)
            em.wait("scalar", "st0", dvb)
        em.wait("scalar", "dv", vbf_ms[HC - 1])
        psum_take(0)
        psum_take(1)
        psum_take(2)

        st_done = [0, 0]
        last_out = None

        def dst_out(oc):
            return (outst[oc % 2][:, :], vbf(oc))

        vcBs = [ar.b(MA, T), ar.b(MA + 2 * KB, T)]
        row_m = ar.f(MB, T)
        row_r = ar.f(MC, T)
        invn = 1.0 / H
        p_2b = 0
        for oc in range(HC):
            vcB = vcBs[oc % 2]
            em.wait("scalar", "pe", p_2b if oc < 2 else p_2b)
            em.op("scalar", lambda e, oc=oc, vcB=vcB: e.copy(vcB, vbf(oc)))
            a_c = em.act_fence()
            em.wait("tensor", "ac", a_c)
            st = (oc == 0)
            sp = (oc == HC - 1)
            em.op("tensor", lambda e, st=st, sp=sp, vcB=vcB:
                  e.matmul(ps[2][0:1, 0:512], cons["onesb"][:, 0:1],
                           vcB[:, 0:512], start=st, stop=sp,
                           skip_group_check=True))
            p_1 = em.op("tensor", lambda e, st=st, sp=sp, vcB=vcB:
                        e.matmul(ps[2][0:1, 512:1024], cons["onesb"][:, 0:1],
                                 vcB[:, 512:1024], start=st, stop=sp,
                                 skip_group_check=True),
                        sem="pe", inc=1)
            em.wait("scalar", "pe", p_1)
            em.op("scalar", lambda e, vcB=vcB: e.square(vcB, vcB))
            a_s = em.act_fence()
            em.wait("tensor", "ac", a_s)
            em.op("tensor", lambda e, st=st, sp=sp, vcB=vcB:
                  e.matmul(ps[0][0:1, 0:512], cons["onesb"][:, 0:1],
                           vcB[:, 0:512], start=st, stop=sp,
                           skip_group_check=True))
            p_2b = em.op("tensor", lambda e, st=st, sp=sp, vcB=vcB:
                         e.matmul(ps[0][0:1, 512:1024], cons["onesb"][:, 0:1],
                                  vcB[:, 512:1024], start=st, stop=sp,
                                  skip_group_check=True),
                         sem="pe", inc=1)
        p_st = p_2b
        em.wait("vector", "pe", p_st)
        em.op("vector", lambda e: e.tensor_scalar(row_m[0:1, :], ps[2][0:1, 0:T],
                                                  invn, None, ALU.mult))
        em.bubble()
        em.op("vector", lambda e: e.tensor_mul(row_r[0:1, :], row_m[0:1, :],
                                               row_m[0:1, :]))
        em.op("vector", lambda e: e.tensor_scalar(row_r[0:1, :], row_r[0:1, :],
                                                  -1.0, None, ALU.mult))
        v_v = em.op("vector", lambda e: e.scalar_tensor_tensor(
            row_r[0:1, :], ps[0][0:1, 0:T], invn, row_r[0:1, :], ALU.mult, ALU.add),
            sem="dv", inc=1)
        em.wait("scalar", "dv", v_v)
        em.op("scalar", lambda e: e.activation(row_r[0:1, :], row_r[0:1, :],
                                               AF.Ln, bias=cons["epsln"][0:1, 0:1]))
        a_q = em.op("scalar", lambda e: e.activation(row_r[0:1, :], row_r[0:1, :],
                                                     AF.Exp, scale=-0.5),
                    sem="ac", inc=1)
        em.wait("tensor", "ac", a_q)
        for j in (0, 1):
            em.op("tensor", lambda e, j=j: e.matmul(
                ps[0][:, j * 512:(j + 1) * 512], cons["ones1"][0:1, :],
                row_m[0:1, j * 512:(j + 1) * 512], start=True, stop=True))
        em.op("tensor", lambda e: e.matmul(
            ps[1][:, 0:512], cons["ones1"][0:1, :],
            row_r[0:1, 0:512], start=True, stop=True))
        p_bc = em.op("tensor", lambda e: e.matmul(
            ps[1][:, 512:1024], cons["ones1"][0:1, :],
            row_r[0:1, 512:1024], start=True, stop=True), sem="pe", inc=1)
        em.wait("vector", "pe", p_bc)
        for oc in range(HC):
            lane = "st%d" % (oc % 2)
            em.wait("vector", lane, st_done[oc % 2])
            em.op("vector", lambda e, oc=oc: e.tensor_sub(outst[oc % 2][:, :],
                                                          vbf(oc), ps[0][:, 0:T]))
            em.op("vector", lambda e, oc=oc:
                  e.tensor_mul(outst[oc % 2][:, :], outst[oc % 2][:, :],
                               ps[1][:, 0:T]))
            last_out = em.dve_fence()
            em.wait("gpsimd", "dv", last_out)
            st_done[oc % 2] = em.dmag(out_d[oc], outst[oc % 2][:, :], lane)
        em.wait("gpsimd", "st0", em.cnt["st0"])
        em.wait("gpsimd", "st1", em.cnt["st1"])

        sems = {}
        with ExitStack() as sctx:
            for sname in sorted(em.cnt.keys()):
                sems[sname] = sctx.enter_context(nc.semaphore(sname))
            block = sctx.enter_context(nc.Block())
            em.finalize(block, sems)

    return nc



def _f32(a):
    return np.ascontiguousarray(a, dtype=np.float32)


def _lhsT(W):
    O, K = W.shape
    return np.ascontiguousarray(
        W.reshape(O // 128, 128, K // 128, 128).transpose(0, 3, 2, 1)
        .reshape(O // 128, 128, K))


def _cols(b):
    return np.ascontiguousarray(b.reshape(-1, 128).T)


def kernel(**inputs):
    f = {k: np.asarray(v) for k, v in inputs.items()}

    upT = _lhsT(_f32(f["L0_attn_up_w"]))
    gateT = _lhsT(_f32(f["L0_attn_gate_w"]))
    lp = _cols(_f32(f["L0_attn_local_point"]))
    laT = _lhsT(_f32(f["L0_attn_local_approx_output"]))
    fw = _f32(f["L0_attn_fuse_weight"])
    fuseT = np.stack([_lhsT(np.ascontiguousarray(fw[:, :, m]))
                      for m in range(6)])
    crow = np.zeros((1, 8), np.float32)
    crow[0, 1:7] = _f32(f["L0_attn_discount_factor"])
    gwT = _lhsT(_f32(f["L0_attn_ffn_gate_w"]))
    uwT = _lhsT(_f32(f["L0_attn_ffn_up_w"]))
    dwT = _lhsT(_f32(f["L0_attn_ffn_down_w"]))
    wbT = _lhsT(_f32(f["L0_ffn_W_base"]))
    bb = _cols(_f32(f["L0_ffn_b_base"]))
    Aall = _f32(f["L0_ffn_lora_A"]).reshape(128, H)
    AaT = _lhsT(Aall).reshape(1, 128, H)
    Ball = np.ascontiguousarray(
        _f32(f["L0_ffn_lora_B"]).transpose(1, 0, 2).reshape(H, 128))
    BaT = _lhsT(Ball)
    rwT = np.ascontiguousarray(
        _f32(f["L0_ffn_router_W"]).T.reshape(HC, 128, 16)
        .transpose(1, 0, 2).reshape(128, HC * 16))
    rb = _f32(f["L0_ffn_router_b"]).reshape(1, 16)
    pr = _f32(f["L0_ffn_domain_prior"]).reshape(1, 16)
    act4 = np.ascontiguousarray(f["L0_ffn_active"].astype(np.int32).reshape(4, 1))
    arr16 = np.broadcast_to(np.arange(16, dtype=np.float32), (4, 16)).copy()
    m16 = np.zeros((16, 128), BF)
    for e_ in range(16):
        m16[e_, e_ * 8:(e_ + 1) * 8] = 1
    wqkvT = _lhsT(np.ascontiguousarray(_f32(f["L1_attn_Wqkv"]).T))
    bqkv = _cols(_f32(f["L1_attn_bqkv"]))
    identb = np.eye(128, dtype=np.float32).astype(BF)

    sharedA = dict(upT=upT, gateT=gateT, lp=lp, laT=laT, fuseT=fuseT, crow=crow,
                   gwT=gwT, uwT=uwT, dwT=dwT, wbT=wbT, bb=bb, AaT=AaT, BaT=BaT,
                   rwT=rwT, rbrow=rb, prow=pr, act4=act4, arr16=arr16, m16=m16,
                   wqkvT=wqkvT, bqkv=bqkv, identb=identb)

    X = _f32(f["L0_attn_hidden_states"])
    in_mapsA = []
    for c in range(NCORES):
        b, t0 = c // 4, (c % 4) * T
        xTc = np.ascontiguousarray(X[b, t0:t0 + T, :].T).reshape(HC, 128, T)
        in_mapsA.append(dict(sharedA, xT=xTc))

    ncA = build_A()
    resA = run_bass_kernel_spmd(ncA, in_mapsA, core_ids=list(range(NCORES)))

    ks = [resA.results[c]["ko"] for c in range(NCORES)]
    vts = [resA.results[c]["vto"] for c in range(NCORES)]
    Kseq, VTb = [], []
    for b in range(2):
        Kb = np.concatenate([ks[4 * b + i] for i in range(4)], axis=2)
        VTseq = np.concatenate([vts[4 * b + i] for i in range(4)],
                               axis=0)
        vtb = np.ascontiguousarray(
            VTseq.reshape(32, 128, 8, 128).transpose(2, 1, 0, 3))
        Kseq.append(np.ascontiguousarray(Kb))
        VTb.append(vtb)

    woT = _lhsT(np.ascontiguousarray(_f32(f["L1_attn_Wo"]).T))
    bo = _cols(_f32(f["L1_attn_bo"]))
    wb1T = _lhsT(_f32(f["L1_ffn_W_base"]))
    bb1 = _cols(_f32(f["L1_ffn_b_base"]))
    A1all = _f32(f["L1_ffn_A"]).reshape(128, H)
    A1T = _lhsT(A1all).reshape(1, 128, H)
    B1all = np.ascontiguousarray(
        _f32(f["L1_ffn_B"]).transpose(1, 0, 2).reshape(H, 128))
    B1T = _lhsT(B1all)
    rw1 = np.ascontiguousarray(
        _f32(f["L1_ffn_W_router"]).T.reshape(HC, 128, 8)
        .transpose(1, 0, 2).reshape(128, HC * 8))
    m8 = np.zeros((8, 128), BF)
    for e_ in range(8):
        m8[e_, e_ * 16:(e_ + 1) * 16] = 1

    sharedB = dict(woT=woT, bo=bo, wb1T=wb1T, bb1=bb1, A1T=A1T, B1T=B1T,
                   rw1=rw1, m8=m8, identb=identb)
    in_mapsB = []
    for c in range(NCORES):
        b = c // 4
        in_mapsB.append(dict(sharedB,
                             q=resA.results[c]["qo"],
                             K=Kseq[b], VT=VTb[b],
                             x3=resA.results[c]["x3o"]))

    ncB = build_B()
    resB = run_bass_kernel_spmd(ncB, in_mapsB, core_ids=list(range(NCORES)))

    out = np.empty((2, 4096, 4096), np.float32)
    for c in range(NCORES):
        b, t0 = c // 4, (c % 4) * T
        o = resB.results[c]["out"]
        out[b, t0:t0 + T, :] = o.reshape(H, T).T
    return out


# version 02
# speedup vs baseline: 1.0392x; 1.0392x over previous
import sys
if "/opt/trn_rl_repo" not in sys.path:
    sys.path.insert(0, "/opt/trn_rl_repo")

from contextlib import ExitStack

import numpy as np
import ml_dtypes

import concourse.bass as bass
import concourse.mybir as mybir
from concourse.bass_utils import run_bass_kernel_spmd

F32 = mybir.dt.float32
BF16 = mybir.dt.bfloat16
I32 = mybir.dt.int32
AF = mybir.ActivationFunctionType
ALU = mybir.AluOpType
AX = mybir.AxisListType
BF = ml_dtypes.bfloat16

NCORES = 8
T = 1024
H = 4096
HC = 32
S = 1024
SC = 8
FF = 9984
FFC = 78
D = 1024
DC = 8
GF = 13
NG = 6
LN_EPS = 1e-5
KB = 1024

ENGS = ("sync", "gpsimd", "scalar", "vector", "tensor")
DEBUG_A = False
DEBUG_B = False
STOP_A = None

XA = 0
XB = 64 * KB
XZ = 128 * KB
MA = 192 * KB
MB = 196 * KB
MC = 200 * KB
ARENA_BYTES = 204 * KB


class Em:

    def __init__(self, nc):
        self.nc = nc
        self.q = {k: [] for k in ENGS}
        self.cnt = {s: 0 for s in ("pe", "ac", "dv")}

    def op(self, eng, fn, sem=None, inc=0):
        if inc:
            self.cnt[sem] = self.cnt.get(sem, 0) + inc
            tgt = self.cnt[sem]

            def wrapped(e, sems, fn=fn, sem=sem, inc=inc):
                fn(e).then_inc(sems[sem], inc)
        else:
            tgt = None

            def wrapped(e, sems, fn=fn):
                fn(e)
        self.q[eng].append(wrapped)
        return tgt

    def wait(self, eng, sem, val):
        if val is None or val <= 0:
            return
        self.q[eng].append(
            lambda e, sems, sem=sem, val=val: e.wait_ge(sems[sem], val))

    def set_fence_cell(self, ap, ap_act=None):
        self._fence = ap
        self._fenca = ap_act

    def bubble(self):
        self.op("vector", lambda e: e.tensor_single_scalar(
            self._fence, self._fence, 1.0, ALU.mult))

    def act_fence(self):
        return self.op("scalar", lambda e: e.mul(self._fenca, self._fenca, 1.0),
                       sem="ac", inc=1)

    def dve_fence(self):
        return self.op("vector", lambda e: e.tensor_single_scalar(
            self._fence, self._fence, 1.0, ALU.mult), sem="dv", inc=1)

    def dmas(self, dst, src, lane):
        return self.op("sync", lambda e, d=dst, s=src: e.dma_start(out=d, in_=s),
                       sem=lane, inc=16)

    def dmag(self, dst, src, lane):
        return self.op("gpsimd", lambda e, d=dst, s=src: e.dma_start(out=d, in_=s),
                       sem=lane, inc=16)

    def mm_group(self, ps_ap, pairs, nsplit=2):
        ncols = ps_ap.shape[-1]
        w = ncols // nsplit
        n = len(pairs)
        tgt = None
        for i, (lh, rh) in enumerate(pairs):
            for j in range(nsplit):
                st = (i == 0)
                sp = (i == n - 1)
                f = (lambda e, p=ps_ap[:, j * w:(j + 1) * w], lh=lh,
                     rh=rh[:, j * w:(j + 1) * w], st=st, sp=sp:
                     e.matmul(p, lh, rh, start=st, stop=sp))
                if sp and j == nsplit - 1:
                    tgt = self.op("tensor", f, sem="pe", inc=1)
                else:
                    self.op("tensor", f)
        return tgt

    def finalize(self, block, sems):
        def mk(name):
            def body(e):
                for f in self.q[name]:
                    f(e, sems)
            return body
        block.sync(mk("sync"))
        block.gpsimd(mk("gpsimd"))
        block.scalar(mk("scalar"))
        block.vector(mk("vector"))
        block.tensor(mk("tensor"))


class Arena:
    def __init__(self, nc, nbytes):
        self.h = nc.alloc_sbuf_tensor("arena", [128, nbytes], mybir.dt.uint8)
        self._f = self.h.ap().bitcast(F32)
        self._b = self.h.ap().bitcast(BF16)

    def f(self, off, n):
        return self._f[:, off // 4: off // 4 + n]

    def b(self, off, n):
        return self._b[:, off // 2: off // 2 + n]


class WStream:

    def __init__(self, em, name, stage_fn, bf_fn, init_dma_wait=None,
                 init_cast_wait=None):
        self.em = em
        self.lanes = (name + "0", name + "1")
        self.stage_fn = stage_fn
        self.bf_fn = bf_fn
        self.i = 0
        self.cast_ms = {}
        self.cons_pe = {}
        self.init_dma_wait = init_dma_wait
        self.init_cast_wait = init_cast_wait

    def job(self, src_ap):
        em = self.em
        j = self.i
        slot = j % 2
        if j == 0 and self.init_dma_wait:
            em.wait("sync", *self.init_dma_wait)
        if j >= 2:
            em.wait("sync", "ac", self.cast_ms[j - 2])
        d = em.dmas(self.stage_fn(slot), src_ap, self.lanes[slot])
        em.wait("scalar", self.lanes[slot], d)
        if j == 0 and self.init_cast_wait:
            em.wait("scalar", *self.init_cast_wait)
        if j >= 2:
            em.wait("scalar", "pe", self.cons_pe.get(j - 2, 0))
        self.cast_ms[j] = em.op(
            "scalar", lambda e, s=slot: e.copy(self.bf_fn(s), self.stage_fn(s)),
            sem="ac", inc=1)
        self.i += 1
        return j, slot, self.cast_ms[j]

    def note_consumed(self, j, pe_ms):
        self.cons_pe[j] = pe_ms


def _pairwise(em, op, src, width, red, out_col, sem=None, inc=0):
    w = width // 2
    em.bubble()
    em.op("vector", lambda e, w=w: e.tensor_tensor(
        red[:, 0:w], src[:, 0:w], src[:, w:2 * w], op))
    while w > 2:
        h = w // 2
        em.bubble()
        em.op("vector", lambda e, h=h, w=w: e.tensor_tensor(
            red[:, 0:h], red[:, 0:h], red[:, h:w], op))
        w = h
    em.bubble()
    return em.op("vector", lambda e: e.tensor_tensor(
        out_col[:, 0:1], red[:, 0:1], red[:, 1:2], op), sem=sem, inc=inc)


def _consts(nc, want):
    out = {}
    for name, shape, dt in want:
        out[name] = nc.alloc_sbuf_tensor("c_" + name, list(shape), dt).ap()
    return out


def _emit_ln(em, ar, cons, ps0, ps1, ps2, src_bf, src_fp, dst_fn, n_oc=HC):
    vcs = [ar.b(MA, T), ar.b(MA + 2 * KB, T)]
    vc2s = [ar.b(MC, T), ar.b(MC + 2 * KB, T)]
    row_m = ar.f(MB, T)
    row_r = ar.f(MC, T)
    invn = 1.0 / H
    p_prev = [em.cnt["pe"], em.cnt["pe"]]
    p_2 = 0
    for oc in range(n_oc):
        vc, vc2 = vcs[oc % 2], vc2s[oc % 2]
        src = src_fp(oc) if src_fp is not None else src_bf(oc)
        em.wait("scalar", "pe", p_prev[oc % 2])
        em.op("scalar", lambda e, vc=vc, src=src: e.copy(vc, src))
        em.op("scalar", lambda e, vc=vc, vc2=vc2: e.square(vc2, vc))
        a_c = em.act_fence()
        em.wait("tensor", "ac", a_c)
        st = (oc == 0)
        sp = (oc == n_oc - 1)
        for j in (0, 1):
            em.op("tensor", lambda e, st=st, sp=sp, vc=vc, j=j:
                  e.matmul(ps2[0:1, j * 512:(j + 1) * 512], cons["onesb"][:, 0:1],
                           vc[:, j * 512:(j + 1) * 512], start=st, stop=sp,
                           skip_group_check=True))
        em.op("tensor", lambda e, st=st, sp=sp, vc2=vc2:
              e.matmul(ps0[0:1, 0:512], cons["onesb"][:, 0:1],
                       vc2[:, 0:512], start=st, stop=sp, skip_group_check=True))
        p_2 = em.op("tensor", lambda e, st=st, sp=sp, vc2=vc2:
                    e.matmul(ps0[0:1, 512:1024], cons["onesb"][:, 0:1],
                             vc2[:, 512:1024], start=st, stop=sp,
                             skip_group_check=True),
                    sem="pe", inc=1)
        p_prev[oc % 2] = p_2
    p_st = p_2
    em.wait("vector", "pe", p_st)
    em.op("vector", lambda e: e.tensor_scalar(row_m[0:1, :], ps2[0:1, 0:T],
                                              invn, None, ALU.mult))
    em.bubble()
    em.op("vector", lambda e: e.tensor_mul(row_r[0:1, :], row_m[0:1, :],
                                           row_m[0:1, :]))
    em.op("vector", lambda e: e.tensor_scalar(row_r[0:1, :], row_r[0:1, :],
                                              -1.0, None, ALU.mult))
    em.op("vector", lambda e: e.scalar_tensor_tensor(
        row_r[0:1, :], ps0[0:1, 0:T], invn, row_r[0:1, :], ALU.mult, ALU.add))
    v_v = em.dve_fence()
    em.wait("scalar", "dv", v_v)
    em.op("scalar", lambda e: e.activation(row_r[0:1, :], row_r[0:1, :],
                                           AF.Ln, bias=cons["epsln"][0:1, 0:1]))
    em.op("scalar", lambda e: e.activation(row_r[0:1, :], row_r[0:1, :],
                                           AF.Exp, scale=-0.5))
    a_q = em.act_fence()
    em.wait("tensor", "ac", a_q)
    for j in (0, 1):
        em.op("tensor", lambda e, j=j: e.matmul(
            ps0[:, j * 512:(j + 1) * 512], cons["ones1"][0:1, :],
            row_m[0:1, j * 512:(j + 1) * 512], start=True, stop=True))
    em.op("tensor", lambda e: e.matmul(
        ps1[:, 0:512], cons["ones1"][0:1, :],
        row_r[0:1, 0:512], start=True, stop=True))
    p_bc = em.op("tensor", lambda e: e.matmul(
        ps1[:, 512:1024], cons["ones1"][0:1, :],
        row_r[0:1, 512:1024], start=True, stop=True), sem="pe", inc=1)
    em.wait("vector", "pe", p_bc)
    last = None
    for oc in range(n_oc):
        dst, ssrc = dst_fn(oc)
        em.op("vector", lambda e, d=dst, s=ssrc: e.tensor_sub(d, s, ps0[:, 0:T]))
        last = em.op("vector", lambda e, d=dst: e.tensor_mul(d, d, ps1[:, 0:T]),
                     sem="dv", inc=1)
    return em.dve_fence()


def _fin(nc, em):
    for lane in list(em.cnt.keys()):
        if lane.startswith(("st", "sp")):
            em.wait("gpsimd", lane, em.cnt[lane])
    sems = {}
    with ExitStack() as sctx:
        for sname in sorted(em.cnt.keys()):
            sems[sname] = sctx.enter_context(nc.semaphore(sname))
        block = sctx.enter_context(nc.Block())
        em.finalize(block, sems)
    return nc



def build_A():
    nc = bass.Bass("TRN2", target_bir_lowering=False,
                   detect_race_conditions=False)

    xT = nc.dram_tensor("xT", [HC, 128, T], F32, kind="ExternalInput").ap()
    upT = nc.dram_tensor("upT", [SC, 128, H], F32, kind="ExternalInput").ap()
    gateT = nc.dram_tensor("gateT", [SC, 128, H], F32, kind="ExternalInput").ap()
    lp_d = nc.dram_tensor("lp", [128, SC], F32, kind="ExternalInput").ap()
    laT = nc.dram_tensor("laT", [HC, 128, S], F32, kind="ExternalInput").ap()
    fuseT = nc.dram_tensor("fuseT", [6, HC, 128, S], F32, kind="ExternalInput").ap()
    crow_d = nc.dram_tensor("crow", [1, 8], F32, kind="ExternalInput").ap()
    gwT = nc.dram_tensor("gwT", [FFC, 128, H], F32, kind="ExternalInput").ap()
    uwT = nc.dram_tensor("uwT", [FFC, 128, H], F32, kind="ExternalInput").ap()
    dwT = nc.dram_tensor("dwT", [HC, 128, FF], F32, kind="ExternalInput").ap()
    wbT = nc.dram_tensor("wbT", [HC, 128, H], F32, kind="ExternalInput").ap()
    bb_d = nc.dram_tensor("bb", [128, HC], F32, kind="ExternalInput").ap()
    AaT = nc.dram_tensor("AaT", [1, 128, H], F32, kind="ExternalInput").ap()
    BaT = nc.dram_tensor("BaT", [HC, 128, 128], F32, kind="ExternalInput").ap()
    rwT_d = nc.dram_tensor("rwT", [128, 512], F32, kind="ExternalInput").ap()
    rb_d = nc.dram_tensor("rbrow", [1, 16], F32, kind="ExternalInput").ap()
    pr_d = nc.dram_tensor("prow", [1, 16], F32, kind="ExternalInput").ap()
    act_d = nc.dram_tensor("act4", [4, 1], I32, kind="ExternalInput").ap()
    arr_d = nc.dram_tensor("arr16", [4, 16], F32, kind="ExternalInput").ap()
    m16_d = nc.dram_tensor("m16", [16, 128], BF16, kind="ExternalInput").ap()
    wqkvT = nc.dram_tensor("wqkvT", [24, 128, H], F32, kind="ExternalInput").ap()
    bqkv_d = nc.dram_tensor("bqkv", [128, 24], F32, kind="ExternalInput").ap()
    idb_d = nc.dram_tensor("identb", [128, 128], BF16, kind="ExternalInput").ap()

    x3o = nc.dram_tensor("x3o", [HC, 128, T], BF16, kind="ExternalOutput").ap()
    qo = nc.dram_tensor("qo", [DC, 128, T], BF16, kind="ExternalOutput").ap()
    ko = nc.dram_tensor("ko", [DC, 128, T], BF16, kind="ExternalOutput").ap()
    vto = nc.dram_tensor("vto", [DC, 128, T], BF16, kind="ExternalOutput").ap()

    act_dram = nc.dram_tensor("act_scr", [FFC, 128, T], BF16, kind="Internal").ap()
    if DEBUG_A:
        dbg_up = nc.dram_tensor("dbg_up", [SC, 128, T], BF16, kind="ExternalOutput").ap()
        dbg_dlt = nc.dram_tensor("dbg_dlt", [SC, 128, T], BF16, kind="ExternalOutput").ap()
        dbg_acc4 = nc.dram_tensor("dbg_acc4", [HC, 128, T], F32, kind="ExternalOutput").ap()
        dbg_tmp = nc.dram_tensor("dbg_tmp", [6, SC, 128, T], BF16, kind="ExternalOutput").ap()
        dbg_acc5 = nc.dram_tensor("dbg_acc5", [HC, 128, T], F32, kind="ExternalOutput").ap()
        dbg_x2 = nc.dram_tensor("dbg_x2", [HC, 128, T], BF16, kind="ExternalOutput").ap()
        dbg_vbf = nc.dram_tensor("dbg_vbf", [HC, 128, T], BF16, kind="ExternalOutput").ap()
        dbg_cb = nc.dram_tensor("dbg_cb", [128, T], F32, kind="ExternalOutput").ap()
        dbg_rt = nc.dram_tensor("dbg_rt", [6, 128, 128], F32, kind="ExternalOutput").ap()
        dbg_hw = nc.dram_tensor("dbg_hw", [128, T], BF16, kind="ExternalOutput").ap()

    with ExitStack() as ctx:
        ar = Arena(nc, ARENA_BYTES)
        ps = [ctx.enter_context(nc.psum_tensor(f"ps{i}", [128, 1024], F32)).ap()
              for i in range(3)]
        pst = [ctx.enter_context(nc.psum_tensor(f"pst{i}", [128, 1024], BF16)).ap()
               for i in range(2)]

        cons = _consts(nc, [
            ("lp", (128, SC), F32), ("nlp", (128, SC), F32),
            ("crow", (1, 8), F32), ("rrow", (1, 8), F32), ("rcol", (128, 8), F32),
            ("bb", (128, HC), F32),
            ("rb", (1, 16), F32), ("pr", (1, 16), F32), ("srow", (1, 16), F32),
            ("act4", (4, 1), I32), ("actf", (4, 1), F32),
            ("arr16", (4, 16), F32), ("eq", (4, 16), F32),
            ("m16", (16, 128), BF16),
            ("srowb", (128, 16), F32), ("mrowb", (128, 16), F32),
            ("bqkv", (128, 24), F32), ("bq32", (128, 8), F32),
            ("ones1", (1, 128), F32), ("onesb", (128, 1), BF16),
            ("identb", (128, 128), BF16),
            ("cols", (128, 8), F32), ("red", (128, 8), F32),
            ("lg", (128, 16), F32), ("emx", (128, 16), F32),
            ("abf", (128, 16), BF16), ("at", (16, 128), BF16),
            ("epsc", (1, 2), F32),
            ("ones4", (4, 128), F32), ("fence", (1, 256), BF16),
        ])
        cons["nmax"] = cons["cols"][:, 0:1]
        cons["sums"] = cons["cols"][:, 1:2]
        cons["rcp"] = cons["cols"][:, 2:3]
        cons["mxc"] = cons["cols"][:, 3:4]
        cons["eps12"] = cons["epsc"][0:1, 0:1]
        cons["epsln"] = cons["epsc"][0:1, 1:2]

        em = Em(nc)
        em.set_fence_cell(cons["fence"][0:1, 0:256], cons["fence"][0:1, 0:256])

        xbf = lambda hc: ar.b(XA + hc * 2048, T)
        acc = lambda oc: ar.f(XA + oc * 4096, T)
        vbf = lambda oc: ar.b(XA + oc * 2048, T)
        x2bf = lambda oc: ar.b(XZ + oc * 2048, T)
        wstage = lambda k, s: ar.f(XB + (k * 2 + s) * 16 * KB, H)
        wbf = lambda k, s: ar.b(XZ + 32 * KB + (k * 2 + s) * 8 * KB, H)
        dlt = lambda s: ar.b(XZ + s * 2048, T)
        upb = lambda s: ar.b(XZ + 16 * KB + s * 2048, T)
        tmp = lambda p, s: ar.b(XZ + (32 + 16 * p) * KB + s * 2048, T)
        fstage = lambda s: ar.f(MA + s * 4 * KB, S)
        fbf = lambda s: ar.b(MC + s * 2 * KB, S)
        xst = [ar.f(MA, T), ar.f(MB, T)]
        sg0 = ar.f(MA, T)
        actout = [ar.b(MC, T), ar.b(MC + 2 * KB, T)]
        actg = lambda g, i: ar.b(XZ + (g % 2) * 26 * KB + i * 2048, T)
        dstage = lambda s: ar.f(XZ + 52 * KB if s == 0 else MA, GF * 128)
        dbf = lambda s: ar.b((XZ + 52 * KB if s == 0 else MA) + 6656, GF * 128)
        wstage7 = lambda s: ar.f(XB + s * 16 * KB, H)
        wbf7 = lambda s: ar.b(XB + 32 * KB + s * 8 * KB, H)
        ba_st = lambda s: ar.f(XB + 48 * KB + s * KB, 128)
        ba_bf = lambda s: ar.b(XB + 50 * KB + s * 512, 128)
        rw_st = ar.f(XB + 52 * KB, 512)
        rw_bf = ar.b(XB + 54 * KB, 512)
        hw = ar.b(MC, T)
        cbx = ar.f(MB, T)
        qch = lambda i: ar.b(XZ + i * 2048, T)
        kch = lambda i: ar.b(XZ + 16 * KB + i * 2048, T)
        vch = lambda i: ar.b(XZ + 32 * KB + i * 2048, T)
        vtch = lambda i: ar.b(XZ + 48 * KB + i * 2048, T)

        ps_free = {0: ("pe", 0), 1: ("pe", 0), 2: ("pe", 0),
                   "tA": ("pe", 0), "tB": ("pe", 0)}

        def psum_take(par):
            sname, sval = ps_free[par]
            em.wait("tensor", sname, sval)

        for dst, src in ((cons["lp"], lp_d), (cons["crow"], crow_d),
                         (cons["bb"], bb_d), (cons["rb"], rb_d),
                         (cons["pr"], pr_d), (cons["act4"], act_d),
                         (cons["arr16"], arr_d), (cons["m16"], m16_d),
                         (cons["bqkv"], bqkv_d)):
            em.dmas(dst[:, :], src[:, :], "cst")
        d_c = em.dmas(cons["identb"][:, :], idb_d[:, :], "cst")
        em.wait("scalar", "cst", d_c)
        em.op("scalar", lambda e: e.mul(cons["nlp"][:, :], cons["lp"][:, :], -1.0))
        em.op("scalar", lambda e: e.mul(cons["bq32"][:, :], cons["bqkv"][:, 0:8],
                                        1.0 / 32.0))
        a_c1 = em.op("scalar", lambda e: e.copy(cons["actf"][:, :], cons["act4"][:, :]),
                     sem="ac", inc=1)
        em.wait("vector", "ac", a_c1)
        em.op("vector", lambda e: e.memset(cons["ones1"][:, :], 1.0))
        em.op("vector", lambda e: e.memset(cons["onesb"][:, :], 1.0))
        em.op("vector", lambda e: e.memset(cons["eps12"][:, :], 1e-12))
        em.op("vector", lambda e: e.memset(cons["ones4"][:, :], 1.0))
        em.op("vector", lambda e: e.memset(cons["rrow"][:, :], 0.0))
        em.op("vector", lambda e: e.memset(cons["fence"][:, :], 1.0))
        em.op("vector", lambda e: e.memset(cons["epsln"][:, :], LN_EPS))
        em.op("vector", lambda e: e.tensor_sub(cons["rrow"][0:1, 0:7],
                                               cons["crow"][0:1, 0:7],
                                               cons["crow"][0:1, 1:8]))
        em.op("vector", lambda e: e.tensor_scalar(cons["eq"][:, :], cons["arr16"][:, :],
                                                  cons["actf"][:, 0:1], None,
                                                  ALU.is_equal))
        v_c1 = em.op("vector", lambda e: e.engine_nop(), sem="dv", inc=1)
        em.wait("scalar", "dv", v_c1)
        em.op("scalar", lambda e: e.activation(cons["srow"][:, :], cons["pr"][:, :],
                                               AF.Ln, bias=cons["eps12"][0:1, 0:1]))
        a_c2 = em.op("scalar", lambda e: e.activation(cons["rrow"][0:1, 0:7],
                                                      cons["rrow"][0:1, 0:7], AF.Exp),
                     sem="ac", inc=1)
        em.wait("vector", "ac", a_c2)
        v_srow = em.op("vector", lambda e: e.tensor_add(cons["srow"][:, :],
                                                        cons["srow"][:, :],
                                                        cons["rb"][:, :]),
                       sem="dv", inc=1)
        em.wait("tensor", "dv", v_srow)
        em.wait("tensor", "ac", a_c2)
        em.op("tensor", lambda e: e.matmul(ps[2][:, 0:8], cons["ones1"][0:1, :],
                                           cons["rrow"][0:1, 0:8], start=True,
                                           stop=True))
        em.op("tensor", lambda e: e.matmul(ps[2][:, 8:24], cons["ones1"][0:1, :],
                                           cons["srow"][0:1, :], start=True, stop=True))
        p_c = em.op("tensor", lambda e: e.matmul(ps[2][:, 32:48], cons["ones4"][:, :],
                                           cons["eq"][:, :], start=True, stop=True),
                    sem="pe", inc=1)
        em.wait("vector", "pe", p_c)
        em.op("vector", lambda e: e.tensor_single_scalar(cons["rcol"][:, :], ps[2][:, 0:8], 1.0, ALU.mult))
        em.op("vector", lambda e: e.tensor_single_scalar(cons["srowb"][:, :], ps[2][:, 8:24], 1.0, ALU.mult))
        v_c2 = em.op("vector", lambda e: e.tensor_single_scalar(cons["mrowb"][:, :],
                                                                ps[2][:, 32:48], 1.0, ALU.mult),
                     sem="dv", inc=1)
        ps_free[2] = ("dv", v_c2)

        xst4 = [ar.f(XB, 4 * T), ar.f(XB + 16 * KB, 4 * T)]
        xc = [0] * 8
        for i in range(8):
            lane = "xl%d" % (i % 2)
            if i >= 2:
                em.wait("sync", "ac", xc[i - 2])
            d = em.dmas(xst4[i % 2].rearrange("p (c t) -> p c t", c=4),
                        xT[4 * i:4 * (i + 1)].rearrange("c p t -> p c t"), lane)
            em.wait("scalar", lane, d)
            xc[i] = em.op("scalar", lambda e, i=i: e.copy(
                ar.b(XA + 4 * i * 2048, 4 * T), xst4[i % 2][:, :]),
                sem="ac", inc=1)
        a_xload = xc[7]
        if STOP_A == "A1":
            return _fin(nc, em)

        ws_a = WStream(em, "wa", lambda s: wstage(0, s)[:, :], lambda s: wbf(0, s)[:, :],
                       init_dma_wait=("ac", a_xload))
        ws_b = WStream(em, "wb", lambda s: wstage(1, s)[:, :], lambda s: wbf(1, s)[:, :])
        em.wait("tensor", "ac", a_xload)
        a_A2_dlt = None
        up_done = [None] * SC
        for sc in range(SC):
            ju, su, cu = ws_a.job(upT[sc])
            jg, sg_, cg = ws_b.job(gateT[sc])
            for (j_, wsx, cast, kind) in ((ju, ws_a, cu, 0), (jg, ws_b, cg, 1)):
                par = (2 * sc + kind) % 2
                em.wait("tensor", "ac", cast)
                psum_take(par)
                w = wbf(kind, su if kind == 0 else sg_)
                pairs = [(w[:, hc * 128:(hc + 1) * 128], xbf(hc)) for hc in range(HC)]
                pm = em.mm_group(ps[par], pairs)
                wsx.note_consumed(j_, pm)
                if kind == 0:
                    em.wait("scalar", "pe", pm)
                    em.op("scalar", lambda e, sc=sc, p=ps[par]: e.copy(upb(sc), p))
                    ev = em.act_fence()
                    ps_free[par] = ("ac", ev)
                    up_done[sc] = ev
                else:
                    em.wait("vector", "pe", pm)
                    ev = em.op("vector", lambda e, sc=sc, p=ps[par]:
                               e.tensor_scalar(dlt(sc), p, cons["nlp"][:, sc:sc + 1],
                                               None, ALU.add),
                               sem="dv", inc=1)
                    ps_free[par] = ("dv", ev)
                    a_A2_dlt = ev

        if DEBUG_A:
            for sc_ in range(SC):
                em.wait("gpsimd", "ac", up_done[sc_])
            em.wait("gpsimd", "dv", a_A2_dlt)
            for sc_ in range(SC):
                em.dmag(dbg_up[sc_], upb(sc_), "st")
                em.dmag(dbg_dlt[sc_], dlt(sc_), "st")
        if STOP_A == "A2":
            return _fin(nc, em)

        spill = [0] * FFC
        v_a3_last = None
        for f in range(FFC):
            jg, sg_slot, cg = ws_a.job(gwT[f])
            ju, su_slot, cu = ws_b.job(uwT[f])
            pg, pu = f % 2, 2
            em.wait("tensor", "ac", cg)
            psum_take(pg)
            pmg = em.mm_group(ps[pg], [(wbf(0, sg_slot)[:, hc * 128:(hc + 1) * 128],
                                        xbf(hc)) for hc in range(HC)])
            ws_a.note_consumed(jg, pmg)
            em.wait("tensor", "ac", cu)
            psum_take(pu)
            pmu = em.mm_group(ps[pu], [(wbf(1, su_slot)[:, hc * 128:(hc + 1) * 128],
                                        xbf(hc)) for hc in range(HC)])
            ws_b.note_consumed(ju, pmu)
            em.wait("scalar", "pe", pmg)
            if v_a3_last is not None:
                em.wait("scalar", "dv", v_a3_last)
            em.op("scalar", lambda e, p=ps[pg]: e.activation(sg0[:, :], p, AF.Silu))
            a_sil = em.act_fence()
            ps_free[pg] = ("ac", a_sil)
            em.wait("vector", "pe", pmu)
            em.wait("vector", "ac", a_sil)
            if f >= 2:
                em.wait("vector", "sp%d" % (f % 2), spill[f - 2])
            em.op("vector",
                  lambda e, p=ps[pu], o=actout[f % 2]: e.tensor_mul(o, sg0[:, :], p))
            v_a = em.dve_fence()
            ps_free[pu] = ("dv", v_a)
            v_a3_last = v_a
            em.wait("gpsimd", "dv", v_a)
            spill[f] = em.dmag(act_dram[f], actout[f % 2], "sp%d" % (f % 2))
        pe_after_A3 = em.cnt["pe"]
        spill_last = spill[FFC - 1]
        spill_lane_last = "sp%d" % ((FFC - 1) % 2)

        if STOP_A == "A3":
            return _fin(nc, em)

        def build_tmp(m, pe_guard):
            dstp, srcp = (m - 1) % 2, (m - 2) % 2
            em.wait("vector", "pe", pe_guard)
            ms = None
            for s in range(SC):
                if m == 1:
                    em.op("vector", lambda e, s=s, dp=dstp:
                          e.tensor_mul(tmp(dp, s), dlt(s), upb(s)))
                else:
                    em.op("vector", lambda e, s=s, dp=dstp, sp=srcp:
                          e.tensor_mul(tmp(dp, s), tmp(sp, s), dlt(s)))
                ms = em.op("vector", lambda e, s=s, dp=dstp, m=m:
                           e.tensor_scalar(tmp(dp, s), tmp(dp, s),
                                           cons["rcol"][:, m - 1:m], None, ALU.mult),
                           sem="dv", inc=1)
            return ms

        def fold_keep(pe_guard):
            em.wait("scalar", "pe", pe_guard)
            em.wait("scalar", "dv", em.cnt["dv"])
            em.wait("vector", "pe", pe_guard)
            ms = None
            for s in range(SC):
                a_ab = em.op("scalar", lambda e, s=s:
                             e.activation(tmp(0, s), dlt(s), AF.Abs),
                             sem="ac", inc=1)
                em.wait("vector", "ac", a_ab)
                em.op("vector", lambda e, s=s:
                      e.tensor_scalar(tmp(0, s), tmp(0, s), 2.5, None, ALU.is_le))
                em.bubble()
                ms = em.op("vector", lambda e, s=s:
                           e.tensor_mul(dlt(s), dlt(s), tmp(0, s)),
                           sem="dv", inc=1)
            return ms

        tmp_dump_ms = {}

        def dump_tmp(m):
            if not DEBUG_A:
                return
            dp = (m - 1) % 2
            em.wait("gpsimd", "dv", tmp_ms[m])
            for s_ in range(SC):
                tmp_dump_ms[m] = em.dmag(dbg_tmp[m - 1, s_], tmp(dp, s_), "st")

        em.wait("vector", "dv", a_A2_dlt)
        for scu in range(SC):
            em.wait("vector", "ac", up_done[scu])
        tmp_ms = {1: build_tmp(1, pe_after_A3)}
        dump_tmp(1)

        ws_f = WStream(em, "wf", lambda s: fstage(s)[:, :], lambda s: fbf(s)[:, :],
                       init_dma_wait=("dv", v_a3_last),
                       init_cast_wait=(spill_lane_last, spill_last))
        groups = [[0, 1], [2, 3], [4, 5], [6]]
        acc_last = [None] * HC
        first_acc_guard = False
        prev_group_pm = pe_after_A3
        for gi, terms in enumerate(groups):
            for t_i in terms:
                if t_i >= 2 and t_i not in tmp_ms:
                    if t_i == 5:
                        fold_keep(prev_group_pm)
                    if DEBUG_A and t_i - 2 in tmp_dump_ms:
                        em.wait("vector", "st", tmp_dump_ms[t_i - 2])
                        em.wait("scalar", "st", tmp_dump_ms[t_i - 2])
                    tmp_ms[t_i] = build_tmp(t_i, prev_group_pm)
                    dump_tmp(t_i)
            for oc in range(HC):
                par = oc % 2
                pairs = []
                consumed = []
                for t_i in terms:
                    src = laT[oc] if t_i == 0 else fuseT[t_i - 1, oc]
                    jj, ss, cc = ws_f.job(src)
                    em.wait("tensor", "ac", cc)
                    w = fbf(ss)
                    if t_i == 0:
                        pairs += [(w[:, s * 128:(s + 1) * 128], upb(s))
                                  for s in range(SC)]
                    else:
                        pairs += [(w[:, s * 128:(s + 1) * 128], tmp((t_i - 1) % 2, s))
                                  for s in range(SC)]
                    consumed.append(jj)
                for t_i in terms:
                    if t_i >= 1:
                        em.wait("tensor", "dv", tmp_ms[t_i])
                psum_take(par)
                pm = em.mm_group(ps[par], pairs)
                for jj in consumed:
                    ws_f.note_consumed(jj, pm)
                if gi == 0:
                    em.wait("scalar", "pe", pm)
                    if not first_acc_guard:
                        em.wait("scalar", "pe", pe_after_A3)
                        first_acc_guard = True
                    em.op("scalar", lambda e, oc=oc, p=ps[par]: e.copy(acc(oc), p))
                    ev = em.act_fence()
                    ps_free[par] = ("ac", ev)
                else:
                    em.wait("vector", "pe", pm)
                    if gi == 1:
                        em.wait("vector", "ac", acc_last[oc])
                    ev = em.op("vector", lambda e, oc=oc, p=ps[par]:
                               e.tensor_add(acc(oc), acc(oc), p),
                               sem="dv", inc=1)
                    ps_free[par] = ("dv", ev)
                acc_last[oc] = ev
            prev_group_pm = em.cnt["pe"]
        pe_after_A4 = em.cnt["pe"]

        if DEBUG_A:
            for oc_ in range(HC):
                em.wait("gpsimd", "dv", acc_last[oc_])
                em.dmag(dbg_acc4[oc_], acc(oc_), "st")
            dbg_gate4 = em.cnt["st"]
            em.wait("vector", "st", dbg_gate4)

        if STOP_A == "A4":
            return _fin(nc, em)

        ws_d = WStream(em, "wd", lambda s: dstage(s)[:, :], lambda s: dbf(s)[:, :],
                       init_dma_wait=("pe", pe_after_A4),
                       init_cast_wait=("pe", pe_after_A4))
        actg_ap = lambda g: ar.b(XZ + (g % 2) * 26 * KB, GF * T)
        group_pe = [pe_after_A4, pe_after_A4]
        dl = [0, 0]
        for g in range(NG):
            em.wait("sync", "pe", group_pe[g % 2])
            hi = (g + 1) * GF
            em.wait("sync", "sp0", 16 * ((hi + 1) // 2))
            em.wait("sync", "sp1", 16 * (hi // 2))
            dl[g % 2] = em.dmas(actg_ap(g).rearrange("p (f t) -> p f t", f=GF),
                                act_dram[g * GF:(g + 1) * GF].rearrange("f p t -> p f t"),
                                "ag%d" % (g % 2))
            em.wait("tensor", "ag%d" % (g % 2), dl[g % 2])
            pm = None
            for oc in range(HC):
                par = oc % 2
                jj, ss, cc = ws_d.job(dwT[oc][:, g * GF * 128:(g + 1) * GF * 128])
                em.wait("tensor", "ac", cc)
                psum_take(par)
                w = dbf(ss)
                pairs = [(w[:, i * 128:(i + 1) * 128], actg(g, i)) for i in range(GF)]
                pm = em.mm_group(ps[par], pairs)
                ws_d.note_consumed(jj, pm)
                em.wait("vector", "pe", pm)
                ev = em.op("vector", lambda e, oc=oc, p=ps[par]:
                           e.tensor_add(acc(oc), acc(oc), p),
                           sem="dv", inc=1)
                ps_free[par] = ("dv", ev)
                acc_last[oc] = ev
            group_pe[g % 2] = pm
        prev_group_pe = pm

        if DEBUG_A:
            for oc_ in range(HC):
                em.wait("gpsimd", "dv", acc_last[oc_])
                em.dmag(dbg_acc5[oc_], acc(oc_), "st")
            dbg_gate5 = em.cnt["st"]
            em.wait("vector", "st", dbg_gate5)

        if STOP_A == "A5":
            return _fin(nc, em)

        xin_sl = [ar.f(XZ, T), ar.f(XZ + 4 * KB, T)]
        add_ms = [None] * HC
        for oc in range(HC):
            lane = "xi%d" % (oc % 2)
            if oc >= 2:
                em.wait("sync", "dv", add_ms[oc - 2])
            else:
                em.wait("sync", "pe", group_pe[0])
            dxi = em.dmas(xin_sl[oc % 2][:, :], xT[oc], lane)
            em.wait("vector", lane, dxi)
            em.wait("vector", "dv", acc_last[oc])
            add_ms[oc] = em.op("vector", lambda e, oc=oc:
                               e.tensor_add(acc(oc), acc(oc), xin_sl[oc % 2][:, :]),
                               sem="dv", inc=1)
        em.wait("scalar", "dv", add_ms[HC - 1])
        psum_take(0)
        psum_take(1)
        psum_take(2)
        em.wait("vector", "pe", prev_group_pe)
        ln1 = _emit_ln(em, ar, cons, ps[0], ps[1], ps[2],
                       src_bf=None, src_fp=lambda oc: acc(oc),
                       dst_fn=lambda oc: (x2bf(oc), acc(oc)))
        ps_free[0] = ("dv", ln1)
        ps_free[1] = ("dv", ln1)
        ps_free[2] = ("dv", ln1)

        if DEBUG_A:
            em.wait("gpsimd", "dv", ln1)
            for oc_ in range(HC):
                em.dmag(dbg_x2[oc_], x2bf(oc_), "st")
            dbg_gate2 = em.cnt["st"]
            em.wait("scalar", "st", dbg_gate2)

        if STOP_A == "A6":
            return _fin(nc, em)

        em.wait("sync", "dv", ln1)
        d_rw = em.dmas(rw_st[:, :], rwT_d[:, :], "cst")
        em.wait("scalar", "cst", d_rw)
        em.wait("scalar", "dv", ln1)
        a_rw = em.op("scalar", lambda e: e.copy(rw_bf[:, :], rw_st[:, :]),
                     sem="ac", inc=1)
        em.wait("tensor", "ac", a_rw)
        em.wait("tensor", "dv", ln1)
        ws7 = WStream(em, "w7", lambda s: wstage7(s)[:, :], lambda s: wbf7(s)[:, :],
                      init_dma_wait=("dv", ln1), init_cast_wait=("dv", ln1))
        jz, sz, cz = ws7.job(AaT[0])
        em.wait("tensor", "ac", cz)
        psum_take(0)
        pairs = [(wbf7(sz)[:, hc * 128:(hc + 1) * 128], x2bf(hc)) for hc in range(HC)]
        pmz = em.mm_group(ps[0], pairs)
        ws7.note_consumed(jz, pmz)

        cb_ms = [None] * 8
        prev_a_ab = 0
        prev_pt = 0
        for tt in range(8):
            psum_take(2)
            pairs = [(x2bf(hc)[:, tt * 128:(tt + 1) * 128],
                      rw_bf[:, hc * 16:(hc + 1) * 16]) for hc in range(HC)]
            pm = em.mm_group(ps[2][:, 0:16], pairs, nsplit=1)
            em.wait("vector", "pe", pm)
            em.wait("vector", "ac", prev_a_ab)
            em.op("vector", lambda e: e.tensor_add(cons["lg"][:, :],
                                                   ps[2][:, 0:16],
                                                   cons["srowb"][:, :]))
            _pairwise(em, ALU.max, cons["lg"], 16, cons["red"], cons["mxc"])
            em.bubble()
            em.op("vector", lambda e: e.tensor_scalar(
                cons["nmax"][:, :], cons["mxc"][:, :], -1.0, None, ALU.mult))
            v_lg = em.dve_fence()
            em.wait("scalar", "dv", v_lg)
            em.op("scalar", lambda e: e.activation(cons["lg"][:, :],
                                                   cons["lg"][:, :], AF.Exp,
                                                   bias=cons["nmax"][:, 0:1]))
            a_ex = em.act_fence()
            em.wait("vector", "ac", a_ex)
            em.op("vector", lambda e: e.tensor_mul(cons["emx"][:, :], cons["lg"][:, :],
                                                   cons["mrowb"][:, :]))
            em.bubble()
            _pairwise(em, ALU.add, cons["emx"], 16, cons["red"], cons["sums"])
            v_sm = em.dve_fence()
            em.wait("scalar", "dv", v_sm)
            em.op("scalar", lambda e: e.activation(cons["rcp"][:, :],
                                                   cons["sums"][:, :], AF.Ln))
            em.op("scalar", lambda e: e.activation(cons["rcp"][:, :],
                                                   cons["rcp"][:, :], AF.Exp,
                                                   scale=-1.0))
            a_rc = em.act_fence()
            em.wait("vector", "ac", a_rc)
            em.op("vector", lambda e: e.tensor_scalar(
                cons["emx"][:, :], cons["emx"][:, :], cons["rcp"][:, 0:1], 2.0,
                ALU.mult, ALU.mult))
            v_al = em.dve_fence()
            em.wait("scalar", "dv", v_al)
            em.wait("scalar", "pe", prev_pt)
            em.op("scalar", lambda e: e.copy(cons["abf"][:, :], cons["emx"][:, :]))
            a_ab = em.act_fence()
            prev_a_ab = a_ab
            em.wait("tensor", "ac", a_ab)
            psum_take("tA")
            pt_ = em.op("tensor", lambda e:
                        e.transpose(pst[0][0:16, 0:128], cons["abf"][:, :],
                                    cons["identb"][:, :]),
                        sem="pe", inc=1)
            prev_pt = pt_
            em.wait("vector", "pe", pt_)
            v_at = em.op("vector", lambda e:
                         e.tensor_single_scalar(cons["at"][:, :],
                                                pst[0][0:16, 0:128], 1.0, ALU.mult),
                         sem="dv", inc=1)
            ps_free["tA"] = ("dv", v_at)
            if DEBUG_A and tt == 0:
                em.wait("gpsimd", "dv", v_at)
                em.dmag(dbg_rt[0][:, 0:16], cons["lg"][:, :], "st")
                em.dmag(dbg_rt[1][:, 0:16], cons["emx"][:, :], "st")
                em.dmag(dbg_rt[2][0:16, :], cons["at"][:, :], "st")
                em.dmag(dbg_rt[3][:, 0:16], cons["srowb"][:, :], "st")
                em.dmag(dbg_rt[4][:, 0:16], cons["mrowb"][:, :], "st")
                dsnap = em.dmag(dbg_rt[5][:, 0:16], cons["abf"][:, :], "st")
                em.wait("scalar", "st", dsnap)
                em.wait("vector", "st", dsnap)
            em.wait("tensor", "dv", v_at)
            pb = em.op("tensor", lambda e:
                       e.matmul(ps[2][:, 128:256], cons["m16"][:, :], cons["at"][:, :],
                                start=True, stop=True),
                       sem="pe", inc=1)
            em.wait("vector", "pe", pb)
            cb_ms[tt] = em.op("vector", lambda e, tt=tt:
                              e.tensor_single_scalar(cbx[:, tt * 128:(tt + 1) * 128],
                                                     ps[2][:, 128:256], 1.0, ALU.mult),
                              sem="dv", inc=1)
            ps_free[2] = ("dv", cb_ms[tt])

        em.wait("vector", "pe", pmz)
        em.wait("vector", "dv", cb_ms[7])
        em.op("vector", lambda e: e.tensor_mul(hw[:, :], cbx[:, :], ps[0][:, :]))
        v_hw = em.dve_fence()
        ps_free[0] = ("dv", v_hw)
        if DEBUG_A:
            em.wait("gpsimd", "dv", v_hw)
            em.dmag(dbg_cb[:, :], cbx[:, :], "st")
            dhw = em.dmag(dbg_hw[:, :], hw[:, :], "st")
            em.wait("vector", "st", dhw)
            em.wait("scalar", "st", dhw)
            em.wait("sync", "st", dhw)

        ws_ba = WStream(em, "wx", lambda s: ba_st(s)[:, :], lambda s: ba_bf(s)[:, :],
                        init_dma_wait=("dv", ln1), init_cast_wait=("dv", ln1))
        vbf_ms = [None] * HC
        first_vbf_guard = False
        for oc in range(HC):
            par = oc % 2
            jw, sw, cw = ws7.job(wbT[oc])
            jb, sb, cb2 = ws_ba.job(BaT[oc])
            em.wait("tensor", "ac", cw)
            em.wait("tensor", "ac", cb2)
            em.wait("tensor", "dv", v_hw)
            psum_take(par)
            pairs = [(wbf7(sw)[:, hc * 128:(hc + 1) * 128], x2bf(hc))
                     for hc in range(HC)]
            pairs.append((ba_bf(sb)[:, :], hw[:, :]))
            pm = em.mm_group(ps[par], pairs)
            ws7.note_consumed(jw, pm)
            ws_ba.note_consumed(jb, pm)
            em.wait("vector", "pe", pm)
            if not first_vbf_guard:
                em.wait("vector", "dv", ln1)
                first_vbf_guard = True
            vbf_ms[oc] = em.op("vector", lambda e, oc=oc, p=ps[par]:
                               e.scalar_tensor_tensor(vbf(oc), p,
                                                      cons["bb"][:, oc:oc + 1],
                                                      x2bf(oc), ALU.add, ALU.add),
                               sem="dv", inc=1)
            ps_free[par] = ("dv", vbf_ms[oc])
        if DEBUG_A:
            for oc_ in range(HC):
                em.wait("gpsimd", "dv", vbf_ms[oc_])
                em.dmag(dbg_vbf[oc_], vbf(oc_), "st")
            dbg_gatev = em.cnt["st"]
            em.wait("vector", "st", dbg_gatev)
        em.wait("scalar", "dv", vbf_ms[HC - 1])
        psum_take(0)
        psum_take(1)
        psum_take(2)
        ln2 = _emit_ln(em, ar, cons, ps[0], ps[1], ps[2],
                       src_bf=lambda oc: vbf(oc), src_fp=None,
                       dst_fn=lambda oc: (vbf(oc), vbf(oc)))
        ps_free[0] = ("dv", ln2)
        ps_free[1] = ("dv", ln2)
        ps_free[2] = ("dv", ln2)
        x3 = vbf

        if STOP_A == "A7":
            return _fin(nc, em)

        qkv_ms = [None] * 24
        first_q_guard = False
        for oc in range(24):
            par = oc % 2
            jw, sw, cw = ws7.job(wqkvT[oc])
            em.wait("tensor", "ac", cw)
            em.wait("tensor", "dv", ln2)
            psum_take(par)
            pairs = [(wbf7(sw)[:, hc * 128:(hc + 1) * 128], x3(hc))
                     for hc in range(HC)]
            pm = em.mm_group(ps[par], pairs)
            ws7.note_consumed(jw, pm)
            em.wait("scalar", "pe", pm)
            if not first_q_guard:
                em.wait("scalar", "dv", ln2)
                first_q_guard = True
            if oc < 8:
                em.op("scalar", lambda e, oc=oc, p=ps[par]:
                      e.activation(qch(oc), p, AF.Identity,
                                   bias=cons["bq32"][:, oc:oc + 1],
                                   scale=1.0 / 32.0))
            elif oc < 16:
                em.op("scalar", lambda e, oc=oc, p=ps[par]:
                      e.activation(kch(oc - 8), p, AF.Identity,
                                   bias=cons["bqkv"][:, oc:oc + 1]))
            else:
                em.op("scalar", lambda e, oc=oc, p=ps[par]:
                      e.activation(vch(oc - 16), p, AF.Identity,
                                   bias=cons["bqkv"][:, oc:oc + 1]))
            ev = em.act_fence()
            qkv_ms[oc] = ev
            ps_free[par] = ("ac", ev)

        tr_ms = None
        for dc in range(DC):
            em.wait("tensor", "ac", qkv_ms[16 + dc])
            for tc in range(8):
                bank = "tA" if (dc * 8 + tc) % 2 == 0 else "tB"
                pp = pst[0] if bank == "tA" else pst[1]
                psum_take(bank)
                pt_ = em.op("tensor", lambda e, dc=dc, tc=tc, pp=pp:
                            e.transpose(pp[:, 0:128],
                                        vch(dc)[:, tc * 128:(tc + 1) * 128],
                                        cons["identb"][:, :]),
                            sem="pe", inc=1)
                em.wait("vector", "pe", pt_)
                tr_ms = em.op("vector", lambda e, dc=dc, tc=tc, pp=pp:
                              e.tensor_single_scalar(vtch(tc)[:, dc * 128:(dc + 1) * 128],
                                                     pp[:, 0:128], 1.0, ALU.mult),
                              sem="dv", inc=1)
                ps_free[bank] = ("dv", tr_ms)

        tr_ms = em.dve_fence()
        em.wait("gpsimd", "dv", ln2)
        for hc in range(HC):
            em.dmag(x3o[hc], x3(hc), "st")
        em.wait("gpsimd", "ac", qkv_ms[15])
        for i in range(DC):
            em.dmag(qo[i], qch(i), "st")
            em.dmag(ko[i], kch(i), "st")
        em.wait("gpsimd", "dv", tr_ms)
        last_st = 0
        for i in range(DC):
            last_st = em.dmag(vto[i], vtch(i), "st")
        em.wait("gpsimd", "st", last_st)
        em.wait("gpsimd", "sp0", em.cnt["sp0"])
        em.wait("gpsimd", "sp1", em.cnt["sp1"])

        sems = {}
        with ExitStack() as sctx:
            for sname in sorted(em.cnt.keys()):
                sems[sname] = sctx.enter_context(nc.semaphore(sname))
            block = sctx.enter_context(nc.Block())
            em.finalize(block, sems)

    return nc



def build_B():
    nc = bass.Bass("TRN2", target_bir_lowering=False,
                   detect_race_conditions=False)

    q_d = nc.dram_tensor("q", [DC, 128, T], BF16, kind="ExternalInput").ap()
    K_d = nc.dram_tensor("K", [DC, 128, 4 * T], BF16, kind="ExternalInput").ap()
    VT_d = nc.dram_tensor("VT", [DC, 128, 32, 128], BF16, kind="ExternalInput").ap()
    x3_d = nc.dram_tensor("x3", [HC, 128, T], BF16, kind="ExternalInput").ap()
    woT = nc.dram_tensor("woT", [HC, 128, D], F32, kind="ExternalInput").ap()
    bo_d = nc.dram_tensor("bo", [128, HC], F32, kind="ExternalInput").ap()
    wb1T = nc.dram_tensor("wb1T", [HC, 128, H], F32, kind="ExternalInput").ap()
    bb1_d = nc.dram_tensor("bb1", [128, HC], F32, kind="ExternalInput").ap()
    A1T = nc.dram_tensor("A1T", [1, 128, H], F32, kind="ExternalInput").ap()
    B1T = nc.dram_tensor("B1T", [HC, 128, 128], F32, kind="ExternalInput").ap()
    rw1_d = nc.dram_tensor("rw1", [128, 256], F32, kind="ExternalInput").ap()
    m8_d = nc.dram_tensor("m8", [8, 128], BF16, kind="ExternalInput").ap()
    idb_d = nc.dram_tensor("identb", [128, 128], BF16, kind="ExternalInput").ap()

    out_d = nc.dram_tensor("out", [HC, 128, T], F32, kind="ExternalOutput").ap()
    if DEBUG_B:
        dbg_x4 = nc.dram_tensor("dbg_x4", [HC, 128, T], BF16, kind="ExternalOutput").ap()
        dbg_ao = nc.dram_tensor("dbg_ao", [DC, 128, T], BF16, kind="ExternalOutput").ap()
        dbg_sc = nc.dram_tensor("dbg_sc", [128, 4 * T], F32, kind="ExternalOutput").ap()
        dbg_pt = nc.dram_tensor("dbg_pt", [2, 128, T], BF16, kind="ExternalOutput").ap()
        dbg_vbfB = nc.dram_tensor("dbg_vbfB", [HC, 128, T], BF16, kind="ExternalOutput").ap()
        dbg_cbB = nc.dram_tensor("dbg_cbB", [128, T], F32, kind="ExternalOutput").ap()

    with ExitStack() as ctx:
        ar = Arena(nc, ARENA_BYTES)
        ps = [ctx.enter_context(nc.psum_tensor(f"ps{i}", [128, 1024], F32)).ap()
              for i in range(3)]
        pst = [ctx.enter_context(nc.psum_tensor(f"pst{i}", [128, 1024], BF16)).ap()
               for i in range(2)]

        cons = _consts(nc, [
            ("bo", (128, HC), F32), ("bb1", (128, HC), F32),
            ("m8", (8, 128), BF16), ("identb", (128, 128), BF16),
            ("ones1", (1, 128), F32), ("onesb", (128, 1), BF16),
            ("cols", (128, 8), F32), ("red", (128, 8), F32),
            ("lg", (128, 8), F32), ("msk", (128, 8), F32), ("emx", (128, 8), F32),
            ("abf", (128, 8), BF16), ("at", (8, 128), BF16),
            ("epsc", (1, 2), F32), ("fence", (1, 256), BF16),
        ])
        cons["nmax"] = cons["cols"][:, 0:1]
        cons["sums"] = cons["cols"][:, 1:2]
        cons["rcp"] = cons["cols"][:, 2:3]
        cons["mxc"] = cons["cols"][:, 3:4]
        cons["m1"] = cons["cols"][:, 4:5]
        cons["epsln"] = cons["epsc"][0:1, 1:2]

        em = Em(nc)
        em.set_fence_cell(cons["fence"][0:1, 0:256], cons["fence"][0:1, 0:256])

        PT = lambda kc: ar.b(XA + kc * 2048, T)
        Kc = lambda dc: ar.b(XB + dc * 8192, 4 * T)
        acc = lambda oc: ar.f(XA + oc * 4096, T)
        vbf = lambda oc: ar.b(XB + oc * 2048, T)
        qc = lambda i: ar.b(XZ + i * 2048, T)
        sc_sl = lambda qt: ar.f(XZ + (16 + 16 * (qt % 2)) * KB, 4 * T)
        pbf = ar.b(XZ + 48 * KB, 4 * T)
        aout = lambda dc: ar.b(XZ + 16 * KB + dc * 2048, T)
        vt_st = lambda s: (ar.b(XZ + 56 * KB, 32 * 128) if s == 0
                           else ar.b(MA, 32 * 128))
        x4 = lambda oc: ar.b(XZ + oc * 2048, T)
        wstage = lambda s: ar.f(XA + s * 16 * KB, H)
        wbf = lambda s: ar.b(XA + 32 * KB + s * 8 * KB, H)
        ba_st = lambda s: ar.f(XA + 48 * KB + s * KB, 128)
        ba_bf = lambda s: ar.b(XA + 50 * KB + s * 512, 128)
        rw_st = ar.f(XA + 52 * KB, 256)
        rw_bf = ar.b(XA + 53 * KB, 256)
        outst = [ar.f(XA + 54 * KB, T), ar.f(XA + 58 * KB, T)]
        hw = ar.b(MC, T)
        cbx = ar.f(MB, T)
        scr_att = ar.f(MA, 2048)

        ps_free = {0: ("pe", 0), 1: ("pe", 0), 2: ("pe", 0),
                   "tA": ("pe", 0), "tB": ("pe", 0)}

        def psum_take(par):
            sname, sval = ps_free[par]
            em.wait("tensor", sname, sval)

        em.dmas(cons["bo"][:, :], bo_d[:, :], "cst")
        em.dmas(cons["bb1"][:, :], bb1_d[:, :], "cst")
        em.dmas(cons["m8"][:, :], m8_d[:, :], "cst")
        d_c = em.dmas(cons["identb"][:, :], idb_d[:, :], "cst")
        em.wait("vector", "cst", d_c)
        em.op("vector", lambda e: e.memset(cons["ones1"][:, :], 1.0))
        em.op("vector", lambda e: e.memset(cons["epsln"][:, :], LN_EPS))
        em.op("vector", lambda e: e.memset(cons["fence"][:, :], 1.0))
        v_c = em.op("vector", lambda e: e.memset(cons["onesb"][:, :], 1.0),
                    sem="dv", inc=1)
        em.wait("scalar", "dv", v_c)

        em.wait("tensor", "dv", v_c)
        for i in range(DC):
            em.dmas(qc(i), q_d[i], "qk")
        qk_total = 0
        for dc in range(DC):
            qk_total = em.dmas(Kc(dc), K_d[dc], "qk")

        pt_ms = [None] * 32
        v_pbh = [0] * 10
        pt_lh = [0] * 9
        sc_ev = [0] * 8

        def emit_scores(qt):
            em.wait("tensor", "qk", qk_total)
            sc = sc_sl(qt)
            first_kb = True
            for kb in range(4):
                par = kb % 2
                psum_take(par)
                pairs = [(qc(dc)[:, qt * 128:(qt + 1) * 128],
                          Kc(dc)[:, kb * T:(kb + 1) * T]) for dc in range(DC)]
                pm = em.mm_group(ps[par], pairs)
                em.wait("scalar", "pe", pm)
                if first_kb:
                    em.wait("scalar", "dv", v_pbh[qt])
                    first_kb = False
                em.op("scalar", lambda e, kb=kb, p=ps[par], sc=sc:
                      e.copy(sc[:, kb * T:(kb + 1) * T], p))
                ev = em.act_fence()
                ps_free[par] = ("ac", ev)
            sc_ev[qt] = ev

        def emit_softmax(qt):
            sc = sc_sl(qt)
            em.wait("vector", "ac", sc_ev[qt])
            _pairwise(em, ALU.max, sc, 4096, scr_att, cons["mxc"])
            em.bubble()
            em.op("vector", lambda e: e.tensor_scalar(
                cons["nmax"][:, :], cons["mxc"][:, :], -1.0, None, ALU.mult))
            v_mx = em.dve_fence()
            em.wait("scalar", "dv", v_mx)
            em.op("scalar", lambda e, sc=sc: e.activation(
                sc[:, :], sc[:, :], AF.Exp, bias=cons["nmax"][:, 0:1]))
            a_ex = em.act_fence()
            em.wait("vector", "ac", a_ex)
            _pairwise(em, ALU.add, sc, 4096, scr_att, cons["sums"])
            v_sm = em.dve_fence()
            em.wait("scalar", "dv", v_sm)
            em.op("scalar", lambda e: e.activation(cons["rcp"][:, :],
                                                   cons["sums"][:, :], AF.Ln))
            em.op("scalar", lambda e: e.activation(cons["rcp"][:, :],
                                                   cons["rcp"][:, :], AF.Exp,
                                                   scale=-1.0))
            a_rc = em.act_fence()
            em.wait("vector", "ac", a_rc)
            em.wait("vector", "pe", pt_lh[qt])
            em.op("vector", lambda e, sc=sc: e.tensor_scalar(
                pbf[:, :], sc[:, :], cons["rcp"][:, 0:1], None, ALU.mult))
            v_pbh[qt + 2] = em.dve_fence()
            if DEBUG_B and qt == 0:
                em.wait("gpsimd", "dv", v_pbh[qt + 2])
                dsc = em.dmag(dbg_sc[:, :], pbf[:, :], "st0")
                em.wait("scalar", "st0", dsc)

        def emit_transposes(qt):
            em.wait("tensor", "dv", v_pbh[qt + 2])
            for kc in range(32):
                bank = "tA" if kc % 2 == 0 else "tB"
                pp = pst[0] if kc % 2 == 0 else pst[1]
                psum_take(bank)
                pt_ = em.op("tensor", lambda e, kc=kc, pp=pp:
                            e.transpose(pp[:, 0:128],
                                        pbf[:, kc * 128:(kc + 1) * 128],
                                        cons["identb"][:, :]),
                            sem="pe", inc=1)
                pt_lh[qt + 1] = pt_
                em.wait("vector", "pe", pt_)
                ms = em.op("vector", lambda e, kc=kc, qt=qt, pp=pp:
                           e.tensor_single_scalar(PT(kc)[:, qt * 128:(qt + 1) * 128],
                                                  pp[:, 0:128], 1.0, ALU.mult),
                           sem="dv", inc=1)
                ps_free[bank] = ("dv", ms)
                pt_ms[kc] = ms
            pf = em.dve_fence()
            for kc in range(32):
                pt_ms[kc] = pf

        emit_scores(0)
        emit_scores(1)
        for qt in range(8):
            emit_softmax(qt)
            emit_transposes(qt)
            if qt + 2 < 8:
                emit_scores(qt + 2)

        if DEBUG_B:
            em.wait("gpsimd", "dv", pt_ms[31])
            em.dmag(dbg_pt[0], PT(0), "st0")
            dpt = em.dmag(dbg_pt[1], PT(31), "st0")
            em.wait("tensor", "st0", dpt)
        pe_after_scores = em.cnt["pe"]

        vt_cast = {}
        ao_ms = [None] * DC
        for dc in range(DC):
            slot = dc % 2
            lane = "vt%d" % slot
            if dc == 1:
                em.wait("sync", "dv", v_pbh[9])
            if dc >= 2:
                em.wait("sync", "pe", vt_cast[dc - 2])
            dvt = em.dmas(vt_st(slot).rearrange("p (kc c) -> p kc c", kc=32),
                          VT_d[dc], lane)
            em.wait("tensor", lane, dvt)
            for kc in range(32):
                em.wait("tensor", "dv", pt_ms[kc]) if dc == 0 else None
            par = dc % 2
            psum_take(par)
            pairs = [(vt_st(slot)[:, kc * 128:(kc + 1) * 128], PT(kc))
                     for kc in range(32)]
            pm = em.mm_group(ps[par], pairs)
            vt_cast[dc] = pm
            em.wait("scalar", "pe", pm)
            em.op("scalar", lambda e, dc=dc, p=ps[par]: e.copy(aout(dc), p))
            ao_ms[dc] = em.act_fence()
            ps_free[par] = ("ac", ao_ms[dc])
        if DEBUG_B:
            em.wait("gpsimd", "ac", ao_ms[DC - 1])
            for dc_ in range(DC):
                dao = em.dmag(dbg_ao[dc_], aout(dc_), "st0")
            em.wait("vector", "st0", dao)

        wo_st = lambda s: ar.f(XZ + s * 4 * KB, D)
        wo_bf = lambda s: ar.b(XZ + 8 * KB + s * 2 * KB, D)
        ws_o = WStream(em, "wo", lambda s: wo_st(s)[:, :], lambda s: wo_bf(s)[:, :],
                       init_dma_wait=("pe", pe_after_scores),
                       init_cast_wait=("pe", pe_after_scores))
        x3_slb = [ar.b(XZ + 32 * KB, T), ar.b(XZ + 34 * KB, T)]
        acc_ms = [None] * HC
        first_acc = False
        for oc in range(HC):
            par = oc % 2
            jw, sw, cw = ws_o.job(woT[oc][:, :])
            em.wait("tensor", "ac", cw)
            em.wait("tensor", "ac", ao_ms[DC - 1])
            psum_take(par)
            pairs = [(wo_bf(sw)[:, dc * 128:(dc + 1) * 128], aout(dc))
                     for dc in range(DC)]
            pm = em.mm_group(ps[par], pairs)
            ws_o.note_consumed(jw, pm)
            lane = "x3%d" % (oc % 2)
            if oc >= 2:
                em.wait("sync", "dv", acc_ms[oc - 2])
            dx3 = em.dmas(x3_slb[oc % 2][:, :], x3_d[oc], lane)
            em.wait("vector", lane, dx3)
            em.wait("vector", "pe", pm)
            if not first_acc:
                first_acc = True
            acc_ms[oc] = em.op("vector", lambda e, oc=oc, p=ps[par]:
                               e.scalar_tensor_tensor(acc(oc), p,
                                                      cons["bo"][:, oc:oc + 1],
                                                      x3_slb[oc % 2][:, :],
                                                      ALU.add, ALU.add),
                               sem="dv", inc=1)
            ps_free[par] = ("dv", acc_ms[oc])
        em.wait("scalar", "dv", acc_ms[HC - 1])
        psum_take(0)
        psum_take(1)
        psum_take(2)
        ln3 = _emit_ln(em, ar, cons, ps[0], ps[1], ps[2],
                       src_bf=None, src_fp=lambda oc: acc(oc),
                       dst_fn=lambda oc: (x4(oc), acc(oc)))
        ps_free[0] = ("dv", ln3)
        ps_free[1] = ("dv", ln3)
        ps_free[2] = ("dv", ln3)
        if DEBUG_B:
            em.wait("gpsimd", "dv", ln3)
            for oc_ in range(HC):
                dx4 = em.dmag(dbg_x4[oc_], x4(oc_), "st0")
            em.wait("vector", "st0", dx4)
            em.wait("scalar", "st0", dx4)

        em.wait("sync", "dv", ln3)
        d_rw = em.dmas(rw_st[:, :], rw1_d[:, :], "cst")
        em.wait("scalar", "cst", d_rw)
        em.wait("scalar", "dv", ln3)
        a_rw = em.op("scalar", lambda e: e.copy(rw_bf[:, :], rw_st[:, :]),
                     sem="ac", inc=1)
        em.wait("tensor", "ac", a_rw)
        em.wait("tensor", "dv", ln3)
        ws_w = WStream(em, "ww", lambda s: wstage(s)[:, :], lambda s: wbf(s)[:, :],
                       init_dma_wait=("dv", ln3), init_cast_wait=("dv", ln3))
        ws_b1 = WStream(em, "wc", lambda s: ba_st(s)[:, :], lambda s: ba_bf(s)[:, :],
                        init_dma_wait=("dv", ln3), init_cast_wait=("dv", ln3))
        jz, sz, cz = ws_w.job(A1T[0])
        em.wait("tensor", "ac", cz)
        psum_take(0)
        pairs = [(wbf(sz)[:, hc * 128:(hc + 1) * 128], x4(hc)) for hc in range(HC)]
        pmz = em.mm_group(ps[0], pairs)
        ws_w.note_consumed(jz, pmz)

        cb_ms = [None] * 8
        prev_a_ab = 0
        prev_pt = 0
        for tt in range(8):
            psum_take(2)
            pairs = [(x4(hc)[:, tt * 128:(tt + 1) * 128],
                      rw_bf[:, hc * 8:(hc + 1) * 8]) for hc in range(HC)]
            pm = em.mm_group(ps[2][:, 0:8], pairs, nsplit=1)
            em.wait("vector", "pe", pm)
            em.wait("vector", "ac", prev_a_ab)
            em.op("vector", lambda e: e.tensor_single_scalar(
                cons["lg"][:, :], ps[2][:, 0:8], 1.0, ALU.mult))
            _pairwise(em, ALU.max, cons["lg"], 8, cons["red"], cons["m1"])
            em.bubble()
            em.op("vector", lambda e: e.tensor_scalar(
                cons["nmax"][:, :], cons["m1"][:, :], -1.0, None, ALU.mult))
            em.op("vector", lambda e: e.tensor_scalar(cons["msk"][:, :],
                                                      cons["lg"][:, :],
                                                      cons["m1"][:, 0:1], None,
                                                      ALU.is_ge))
            em.bubble()
            em.op("vector", lambda e: e.scalar_tensor_tensor(
                cons["msk"][:, :], cons["msk"][:, :], -1e30, cons["lg"][:, :],
                ALU.mult, ALU.add))
            em.bubble()
            _pairwise(em, ALU.max, cons["msk"], 8, cons["red"], cons["m1"])
            em.bubble()
            em.op("vector", lambda e: e.tensor_scalar(cons["msk"][:, :],
                                                      cons["lg"][:, :],
                                                      cons["m1"][:, 0:1], None,
                                                      ALU.is_ge))
            em.bubble()
            em.bubble()
            v_m2 = em.dve_fence()
            em.wait("scalar", "dv", v_m2)
            em.op("scalar", lambda e: e.activation(cons["lg"][:, :],
                                                   cons["lg"][:, :], AF.Exp,
                                                   bias=cons["nmax"][:, 0:1]))
            a_ex = em.act_fence()
            em.wait("vector", "ac", a_ex)
            em.op("vector", lambda e: e.tensor_mul(cons["emx"][:, :], cons["lg"][:, :],
                                                   cons["msk"][:, :]))
            em.bubble()
            _pairwise(em, ALU.add, cons["emx"], 8, cons["red"], cons["sums"])
            v_sm = em.dve_fence()
            em.wait("scalar", "dv", v_sm)
            em.op("scalar", lambda e: e.activation(cons["rcp"][:, :],
                                                   cons["sums"][:, :], AF.Ln))
            em.op("scalar", lambda e: e.activation(cons["rcp"][:, :],
                                                   cons["rcp"][:, :], AF.Exp,
                                                   scale=-1.0))
            a_rc = em.act_fence()
            em.wait("vector", "ac", a_rc)
            em.op("vector", lambda e: e.tensor_scalar(
                cons["emx"][:, :], cons["emx"][:, :], cons["rcp"][:, 0:1], 2.0,
                ALU.mult, ALU.mult))
            v_al = em.dve_fence()
            em.wait("scalar", "dv", v_al)
            em.wait("scalar", "pe", prev_pt)
            em.op("scalar", lambda e: e.copy(cons["abf"][:, :], cons["emx"][:, :]))
            a_ab = em.act_fence()
            prev_a_ab = a_ab
            em.wait("tensor", "ac", a_ab)
            psum_take("tA")
            pt_ = em.op("tensor", lambda e:
                        e.transpose(pst[0][0:8, 0:128], cons["abf"][:, :],
                                    cons["identb"][:, :]),
                        sem="pe", inc=1)
            prev_pt = pt_
            em.wait("vector", "pe", pt_)
            em.op("vector", lambda e:
                  e.tensor_single_scalar(cons["at"][:, :],
                                         pst[0][0:8, 0:128], 1.0, ALU.mult))
            v_at = em.dve_fence()
            ps_free["tA"] = ("dv", v_at)
            em.wait("tensor", "dv", v_at)
            pb = em.op("tensor", lambda e:
                       e.matmul(ps[2][:, 128:256], cons["m8"][:, :], cons["at"][:, :],
                                start=True, stop=True),
                       sem="pe", inc=1)
            em.wait("vector", "pe", pb)
            cb_ms[tt] = em.op("vector", lambda e, tt=tt:
                              e.tensor_single_scalar(cbx[:, tt * 128:(tt + 1) * 128],
                                                     ps[2][:, 128:256], 1.0, ALU.mult),
                              sem="dv", inc=1)
            ps_free[2] = ("dv", cb_ms[tt])

        em.wait("vector", "pe", pmz)
        em.wait("vector", "dv", cb_ms[7])
        em.op("vector", lambda e: e.tensor_mul(hw[:, :], cbx[:, :], ps[0][:, :]))
        v_hw = em.dve_fence()
        ps_free[0] = ("dv", v_hw)
        if DEBUG_B:
            em.wait("gpsimd", "dv", v_hw)
            dcb = em.dmag(dbg_cbB[:, :], cbx[:, :], "st0")
            em.wait("vector", "st0", dcb)
            em.wait("scalar", "st0", dcb)

        vbf_ms = [None] * HC
        first_v = False
        for oc in range(HC):
            par = oc % 2
            jw, sw, cw = ws_w.job(wb1T[oc])
            jb, sb, cb2 = ws_b1.job(B1T[oc])
            em.wait("tensor", "ac", cw)
            em.wait("tensor", "ac", cb2)
            em.wait("tensor", "dv", v_hw)
            psum_take(par)
            pairs = [(wbf(sw)[:, hc * 128:(hc + 1) * 128], x4(hc))
                     for hc in range(HC)]
            pairs.append((ba_bf(sb)[:, :], hw[:, :]))
            pm = em.mm_group(ps[par], pairs)
            ws_w.note_consumed(jw, pm)
            ws_b1.note_consumed(jb, pm)
            em.wait("vector", "pe", pm)
            if not first_v:
                em.wait("vector", "dv", ln3)
                first_v = True
            vbf_ms[oc] = em.op("vector", lambda e, oc=oc, p=ps[par]:
                               e.scalar_tensor_tensor(vbf(oc), p,
                                                      cons["bb1"][:, oc:oc + 1],
                                                      x4(oc), ALU.add, ALU.add),
                               sem="dv", inc=1)
            ps_free[par] = ("dv", vbf_ms[oc])
        if DEBUG_B:
            em.wait("gpsimd", "dv", vbf_ms[HC - 1])
            for oc_ in range(HC):
                dvb = em.dmag(dbg_vbfB[oc_], vbf(oc_), "st0")
            em.wait("vector", "st0", dvb)
            em.wait("scalar", "st0", dvb)
        em.wait("scalar", "dv", vbf_ms[HC - 1])
        psum_take(0)
        psum_take(1)
        psum_take(2)

        st_done = [0, 0]
        last_out = None

        def dst_out(oc):
            return (outst[oc % 2][:, :], vbf(oc))

        vcBs = [ar.b(MA, T), ar.b(MA + 2 * KB, T)]
        row_m = ar.f(MB, T)
        row_r = ar.f(MC, T)
        invn = 1.0 / H
        p_2b = 0
        for oc in range(HC):
            vcB = vcBs[oc % 2]
            em.wait("scalar", "pe", p_2b if oc < 2 else p_2b)
            em.op("scalar", lambda e, oc=oc, vcB=vcB: e.copy(vcB, vbf(oc)))
            a_c = em.act_fence()
            em.wait("tensor", "ac", a_c)
            st = (oc == 0)
            sp = (oc == HC - 1)
            em.op("tensor", lambda e, st=st, sp=sp, vcB=vcB:
                  e.matmul(ps[2][0:1, 0:512], cons["onesb"][:, 0:1],
                           vcB[:, 0:512], start=st, stop=sp,
                           skip_group_check=True))
            p_1 = em.op("tensor", lambda e, st=st, sp=sp, vcB=vcB:
                        e.matmul(ps[2][0:1, 512:1024], cons["onesb"][:, 0:1],
                                 vcB[:, 512:1024], start=st, stop=sp,
                                 skip_group_check=True),
                        sem="pe", inc=1)
            em.wait("scalar", "pe", p_1)
            em.op("scalar", lambda e, vcB=vcB: e.square(vcB, vcB))
            a_s = em.act_fence()
            em.wait("tensor", "ac", a_s)
            em.op("tensor", lambda e, st=st, sp=sp, vcB=vcB:
                  e.matmul(ps[0][0:1, 0:512], cons["onesb"][:, 0:1],
                           vcB[:, 0:512], start=st, stop=sp,
                           skip_group_check=True))
            p_2b = em.op("tensor", lambda e, st=st, sp=sp, vcB=vcB:
                         e.matmul(ps[0][0:1, 512:1024], cons["onesb"][:, 0:1],
                                  vcB[:, 512:1024], start=st, stop=sp,
                                  skip_group_check=True),
                         sem="pe", inc=1)
        p_st = p_2b
        em.wait("vector", "pe", p_st)
        em.op("vector", lambda e: e.tensor_scalar(row_m[0:1, :], ps[2][0:1, 0:T],
                                                  invn, None, ALU.mult))
        em.bubble()
        em.op("vector", lambda e: e.tensor_mul(row_r[0:1, :], row_m[0:1, :],
                                               row_m[0:1, :]))
        em.op("vector", lambda e: e.tensor_scalar(row_r[0:1, :], row_r[0:1, :],
                                                  -1.0, None, ALU.mult))
        v_v = em.op("vector", lambda e: e.scalar_tensor_tensor(
            row_r[0:1, :], ps[0][0:1, 0:T], invn, row_r[0:1, :], ALU.mult, ALU.add),
            sem="dv", inc=1)
        em.wait("scalar", "dv", v_v)
        em.op("scalar", lambda e: e.activation(row_r[0:1, :], row_r[0:1, :],
                                               AF.Ln, bias=cons["epsln"][0:1, 0:1]))
        a_q = em.op("scalar", lambda e: e.activation(row_r[0:1, :], row_r[0:1, :],
                                                     AF.Exp, scale=-0.5),
                    sem="ac", inc=1)
        em.wait("tensor", "ac", a_q)
        for j in (0, 1):
            em.op("tensor", lambda e, j=j: e.matmul(
                ps[0][:, j * 512:(j + 1) * 512], cons["ones1"][0:1, :],
                row_m[0:1, j * 512:(j + 1) * 512], start=True, stop=True))
        em.op("tensor", lambda e: e.matmul(
            ps[1][:, 0:512], cons["ones1"][0:1, :],
            row_r[0:1, 0:512], start=True, stop=True))
        p_bc = em.op("tensor", lambda e: e.matmul(
            ps[1][:, 512:1024], cons["ones1"][0:1, :],
            row_r[0:1, 512:1024], start=True, stop=True), sem="pe", inc=1)
        em.wait("vector", "pe", p_bc)
        for oc in range(HC):
            lane = "st%d" % (oc % 2)
            em.wait("vector", lane, st_done[oc % 2])
            em.op("vector", lambda e, oc=oc: e.tensor_sub(outst[oc % 2][:, :],
                                                          vbf(oc), ps[0][:, 0:T]))
            em.op("vector", lambda e, oc=oc:
                  e.tensor_mul(outst[oc % 2][:, :], outst[oc % 2][:, :],
                               ps[1][:, 0:T]))
            last_out = em.dve_fence()
            em.wait("gpsimd", "dv", last_out)
            st_done[oc % 2] = em.dmag(out_d[oc], outst[oc % 2][:, :], lane)
        em.wait("gpsimd", "st0", em.cnt["st0"])
        em.wait("gpsimd", "st1", em.cnt["st1"])

        sems = {}
        with ExitStack() as sctx:
            for sname in sorted(em.cnt.keys()):
                sems[sname] = sctx.enter_context(nc.semaphore(sname))
            block = sctx.enter_context(nc.Block())
            em.finalize(block, sems)

    return nc



def _f32(a):
    return np.ascontiguousarray(a, dtype=np.float32)


def _lhsT(W):
    O, K = W.shape
    return np.ascontiguousarray(
        W.reshape(O // 128, 128, K // 128, 128).transpose(0, 3, 2, 1)
        .reshape(O // 128, 128, K))


def _cols(b):
    return np.ascontiguousarray(b.reshape(-1, 128).T)


def kernel(**inputs):
    f = {k: np.asarray(v) for k, v in inputs.items()}

    upT = _lhsT(_f32(f["L0_attn_up_w"]))
    gateT = _lhsT(_f32(f["L0_attn_gate_w"]))
    lp = _cols(_f32(f["L0_attn_local_point"]))
    laT = _lhsT(_f32(f["L0_attn_local_approx_output"]))
    fw = _f32(f["L0_attn_fuse_weight"])
    fuseT = np.stack([_lhsT(np.ascontiguousarray(fw[:, :, m]))
                      for m in range(6)])
    crow = np.zeros((1, 8), np.float32)
    crow[0, 1:7] = _f32(f["L0_attn_discount_factor"])
    gwT = _lhsT(_f32(f["L0_attn_ffn_gate_w"]))
    uwT = _lhsT(_f32(f["L0_attn_ffn_up_w"]))
    dwT = _lhsT(_f32(f["L0_attn_ffn_down_w"]))
    wbT = _lhsT(_f32(f["L0_ffn_W_base"]))
    bb = _cols(_f32(f["L0_ffn_b_base"]))
    Aall = _f32(f["L0_ffn_lora_A"]).reshape(128, H)
    AaT = _lhsT(Aall).reshape(1, 128, H)
    Ball = np.ascontiguousarray(
        _f32(f["L0_ffn_lora_B"]).transpose(1, 0, 2).reshape(H, 128))
    BaT = _lhsT(Ball)
    rwT = np.ascontiguousarray(
        _f32(f["L0_ffn_router_W"]).T.reshape(HC, 128, 16)
        .transpose(1, 0, 2).reshape(128, HC * 16))
    rb = _f32(f["L0_ffn_router_b"]).reshape(1, 16)
    pr = _f32(f["L0_ffn_domain_prior"]).reshape(1, 16)
    act4 = np.ascontiguousarray(f["L0_ffn_active"].astype(np.int32).reshape(4, 1))
    arr16 = np.broadcast_to(np.arange(16, dtype=np.float32), (4, 16)).copy()
    m16 = np.zeros((16, 128), BF)
    for e_ in range(16):
        m16[e_, e_ * 8:(e_ + 1) * 8] = 1
    wqkvT = _lhsT(np.ascontiguousarray(_f32(f["L1_attn_Wqkv"]).T))
    bqkv = _cols(_f32(f["L1_attn_bqkv"]))
    identb = np.eye(128, dtype=np.float32).astype(BF)

    sharedA = dict(upT=upT, gateT=gateT, lp=lp, laT=laT, fuseT=fuseT, crow=crow,
                   gwT=gwT, uwT=uwT, dwT=dwT, wbT=wbT, bb=bb, AaT=AaT, BaT=BaT,
                   rwT=rwT, rbrow=rb, prow=pr, act4=act4, arr16=arr16, m16=m16,
                   wqkvT=wqkvT, bqkv=bqkv, identb=identb)

    X = _f32(f["L0_attn_hidden_states"])
    in_mapsA = []
    for c in range(NCORES):
        b, t0 = c // 4, (c % 4) * T
        xTc = np.ascontiguousarray(X[b, t0:t0 + T, :].T).reshape(HC, 128, T)
        in_mapsA.append(dict(sharedA, xT=xTc))

    ncA = build_A()
    resA = run_bass_kernel_spmd(ncA, in_mapsA, core_ids=list(range(NCORES)))

    ks = [resA.results[c]["ko"] for c in range(NCORES)]
    vts = [resA.results[c]["vto"] for c in range(NCORES)]
    Kseq, VTb = [], []
    for b in range(2):
        Kb = np.concatenate([ks[4 * b + i] for i in range(4)], axis=2)
        VTseq = np.concatenate([vts[4 * b + i] for i in range(4)],
                               axis=0)
        vtb = np.ascontiguousarray(
            VTseq.reshape(32, 128, 8, 128).transpose(2, 1, 0, 3))
        Kseq.append(np.ascontiguousarray(Kb))
        VTb.append(vtb)

    woT = _lhsT(np.ascontiguousarray(_f32(f["L1_attn_Wo"]).T))
    bo = _cols(_f32(f["L1_attn_bo"]))
    wb1T = _lhsT(_f32(f["L1_ffn_W_base"]))
    bb1 = _cols(_f32(f["L1_ffn_b_base"]))
    A1all = _f32(f["L1_ffn_A"]).reshape(128, H)
    A1T = _lhsT(A1all).reshape(1, 128, H)
    B1all = np.ascontiguousarray(
        _f32(f["L1_ffn_B"]).transpose(1, 0, 2).reshape(H, 128))
    B1T = _lhsT(B1all)
    rw1 = np.ascontiguousarray(
        _f32(f["L1_ffn_W_router"]).T.reshape(HC, 128, 8)
        .transpose(1, 0, 2).reshape(128, HC * 8))
    m8 = np.zeros((8, 128), BF)
    for e_ in range(8):
        m8[e_, e_ * 16:(e_ + 1) * 16] = 1

    sharedB = dict(woT=woT, bo=bo, wb1T=wb1T, bb1=bb1, A1T=A1T, B1T=B1T,
                   rw1=rw1, m8=m8, identb=identb)
    in_mapsB = []
    for c in range(NCORES):
        b = c // 4
        in_mapsB.append(dict(sharedB,
                             q=resA.results[c]["qo"],
                             K=Kseq[b], VT=VTb[b],
                             x3=resA.results[c]["x3o"]))

    ncB = build_B()
    resB = run_bass_kernel_spmd(ncB, in_mapsB, core_ids=list(range(NCORES)))

    out = np.empty((2, 4096, 4096), np.float32)
    for c in range(NCORES):
        b, t0 = c // 4, (c % 4) * T
        o = resB.results[c]["out"]
        out[b, t0:t0 + T, :] = o.reshape(H, T).T
    return out
